# version 14
# baseline (speedup 1.0000x reference)
import math
from contextlib import ExitStack

import numpy as np
import ml_dtypes

import concourse.bass as bass
import concourse.mybir as mybir
from concourse.bass_utils import run_bass_kernel_spmd

F32 = mybir.dt.float32
BF16 = mybir.dt.bfloat16
U8 = mybir.dt.uint8
AF = mybir.ActivationFunctionType
ALU = mybir.AluOpType

B, T, D, NH, HD, II = 2, 8192, 2048, 16, 128, 8192
TLOC = 2048
NCC = D // 128
NKC = T // 128
NGI = II // 128
SCL = 1.0 / math.sqrt(HD)
EXP_BIAS = -5.0
EPS = 1e-5

ENGS = ("sync", "tensor", "vector", "scalar", "gpsimd")


class SemT:
    def __init__(self, ctx, nc, name):
        self.h = ctx.enter_context(nc.semaphore(name))
        self.v = 0

    def bump(self, n=1):
        self.v += n
        return self.v


class Prog:
    def __init__(self):
        self.q = {e: [] for e in ENGS}

    def emit(self, eng, fn):
        self.q[eng].append(fn)

    def wait(self, eng, sem, val):
        if val > 0:
            self.emit(eng, lambda e, s=sem.h, v=val: e.wait_ge(s, v))

    def op(self, eng, fn, inc=None, n=1):
        if inc is None:
            self.emit(eng, fn)
            return None
        tgt = inc.bump(n)
        self.emit(eng, lambda e, f=fn, s=inc.h, m=n: f(e).then_inc(s, m))
        return tgt

    def flush(self, block):
        for eng in ENGS:
            ops = self.q[eng]
            if not ops:
                continue
            dec = getattr(block, eng)

            def section(e, _ops=ops):
                for f in _ops:
                    f(e)
            dec(section)

    def counts(self):
        return {e: len(self.q[e]) for e in ENGS}


def build_program(verbose=False, upto="full", q_ntq=None, q_nh=None):
    nc = bass.Bass("TRN2", target_bir_lowering=False)

    def dram(name, shape, dt, kind):
        return nc.dram_tensor(name, shape, dt, kind=kind).ap()

    xt = dram("xt", [D, T], F32, "ExternalInput")
    xq = dram("xq", [D, TLOC], F32, "ExternalInput")
    wq = dram("wq", [D, D], F32, "ExternalInput")
    wk = dram("wk", [D, D], F32, "ExternalInput")
    wv = dram("wv", [D, D], F32, "ExternalInput")
    wo = dram("wo", [D, D], F32, "ExternalInput")
    wg = dram("wg", [D, II], U8, "ExternalInput")
    wu = dram("wu", [D, II], U8, "ExternalInput")
    wd = dram("wd", [II, D], U8, "ExternalInput")
    gs = dram("gs", [128, NGI], F32, "ExternalInput")
    us = dram("us", [128, NGI], F32, "ExternalInput")
    ds = dram("ds", [128, NCC], F32, "ExternalInput")
    ropek = dram("ropek", [128, T], F32, "ExternalInput")
    ropeq = dram("ropeq", [128, TLOC], F32, "ExternalInput")
    consts = dram("consts", [128, 4], F32, "ExternalInput")
    onec = dram("onec", [128, 1], BF16, "ExternalInput")
    oner = dram("oner", [1, 128], F32, "ExternalInput")

    kspill = dram("kspill", [128, NH * T], BF16, "Internal")
    vspill = dram("vspill", [128, NH, NKC, 128], BF16, "Internal")
    qspill = dram("qspill", [128, NH * TLOC], BF16, "Internal")
    x2spill = dram("x2spill", [128, NCC, TLOC], F32, "Internal")
    x2bspill = dram("x2bspill", [128, NCC, TLOC], BF16, "Internal")
    y = dram("y", [128, NCC, TLOC], F32, "ExternalOutput")

    p = Prog()

    with ExitStack() as top:
        def sem(name):
            return SemT(top, nc, name)

        def g_sb(name, shape, dt):
            return top.enter_context(nc.sbuf_tensor(name, shape, dt))

        consts_t = g_sb("consts_t", [128, 4], F32)
        onec_t = g_sb("onec_t", [128, 1], BF16)
        oner_t = g_sb("oner_t", [1, 128], F32)
        gs_t = g_sb("gs_t", [128, NGI], F32)
        us_t = g_sb("us_t", [128, NGI], F32)
        ds_t = g_sb("ds_t", [128, NCC], F32)

        s_init = sem("s_init")
        for dst, src in ((consts_t, consts), (onec_t, onec), (oner_t, oner),
                         (gs_t, gs), (us_t, us), (ds_t, ds)):
            p.op("sync", lambda e, d=dst, s=src: e.dma_start(out=d[:], in_=s),
                 inc=s_init, n=16)
        fence_init = [(s_init, s_init.v)]
        c_expbias = consts_t[:, 0:1]
        c_m128 = consts_t[:, 1:2]
        c_eps = consts_t[0:1, 2:3]

        s_wl = [sem("s_wl0"), sem("s_wl1")]
        s_wc = sem("s_wc")

        def build_w_slab(src, slab, stg, s_stgfree, wait_first=()):
            npc = 8
            cols = (NCC * D) // npc
            src3 = src.rearrange("(cc p) m -> p cc m", p=128)
            for i in range(npc):
                w0 = i * cols
                par = i % 2
                p.wait("sync", s_wc, max(0, s_wc.v - 1))
                if i == 0:
                    for s, v in wait_first:
                        p.wait("sync", s, v)
                    if s_stgfree is not None:
                        p.wait("sync", s_stgfree[0], s_stgfree[0].v)
                        p.wait("sync", s_stgfree[1], s_stgfree[1].v)
                p.op("sync", lambda e, a=stg[par], b=src3, i_=i, c=cols:
                     e.dma_start(
                         out=a[:, 0:c].rearrange("p (cc m) -> p cc m", cc=2),
                         in_=b[:, 2 * i_:2 * i_ + 2, :]),
                     inc=s_wl[par], n=16)
                p.wait("vector", s_wl[par], s_wl[par].v)
                if i == 0:
                    for s, v in wait_first:
                        p.wait("vector", s, v)
                p.op("vector", lambda e, o=w0, c=cols, a=stg[par], sl=slab:
                     e.tensor_copy(sl[:, o:o + c], a[:, 0:c]), inc=s_wc)
            return [(s_wc, s_wc.v)]

        with ExitStack() as pj:
            def pj_sb(name, shape, dt):
                return pj.enter_context(nc.sbuf_tensor(name, shape, dt))
            xf = [pj_sb("xf0", [128, 4096], F32), pj_sb("xf1", [128, 4096], F32)]
            s_xf = [sem("s_xf0"), sem("s_xf1")]
            s_xfc = [sem("s_xfc0"), sem("s_xfc1")]
            wslab = pj_sb("wslab", [128, NCC * D], BF16)

            def proj_phase(tag, xdram, ttot, ropetab, spill, prev_fence):
                ntq = ttot // 512
                if q_ntq is not None and tag == "q":
                    ntq = q_ntq
                with ExitStack() as ph:
                    def sb(name, shape, dt):
                        return ph.enter_context(
                            nc.sbuf_tensor(tag + name, shape, dt))
                    xb = [sb("xb0", [128, NCC * 512], BF16),
                          sb("xb1", [128, NCC * 512], BF16)]
                    stage = [sb("st0", [128, 8 * 512], BF16),
                             sb("st1", [128, 8 * 512], BF16)]
                    tsh = [sb("tsh0", [64, 512], F32), sb("tsh1", [64, 512], F32)]
                    tm1 = [sb("tm10", [64, 512], F32), sb("tm11", [64, 512], F32)]
                    tm2 = [sb("tm20", [64, 512], F32), sb("tm21", [64, 512], F32)]
                    sstrip = [sb("ss0", [64, 512], F32), sb("ss1", [64, 512], F32)]
                    rtab = sb("rtab", [128, ttot], F32)
                    psK = [ph.enter_context(
                               nc.psum_tensor(tag + "psK0", [128, 512], F32)),
                           ph.enter_context(
                               nc.psum_tensor(tag + "psK1", [128, 512], F32))]

                    s_tab = sem(tag + "s_tab")
                    s_mm = sem(tag + "s_mm")
                    s_shA = sem(tag + "s_shA")
                    s_t1 = sem(tag + "s_t1")
                    s_ropeD = sem(tag + "s_ropeD")
                    s_ropeA = sem(tag + "s_ropeA")
                    s_strip = sem(tag + "s_strip")
                    s_sp = [sem(tag + "s_sp0"), sem(tag + "s_sp1")]
                    s_xbf = sem(tag + "s_xbf")

                    for s, v in prev_fence:
                        p.wait("sync", s, v)
                    p.op("sync", lambda e: e.dma_start(out=rtab[:], in_=ropetab),
                         inc=s_tab, n=16)
                    TABV = s_tab.v

                    x3 = xdram.rearrange("(cc p) t -> p cc t", p=128)
                    first_dve_tab = True

                    for tq in range(ntq):
                        par = tq % 2
                        for hf in range(2):
                            xpar = (2 * tq + hf) % 2
                            p.wait("sync", s_xfc[xpar], s_xfc[xpar].v)
                            t0 = tq * 512 + hf * 256
                            p.op("sync", lambda e, a=xf[xpar], b=x3, o=t0:
                                 e.dma_start(
                                     out=a[:].rearrange("p (cc t) -> p cc t",
                                                        cc=NCC),
                                     in_=b[:, :, o:o + 256]),
                                 inc=s_xf[xpar], n=16)
                            p.wait("vector", s_xf[xpar], s_xf[xpar].v)
                            p.wait("vector", s_xbf, max(0, s_xbf.v - 1))
                            p.op("vector", lambda e, a=xb[par], b=xf[xpar], hh=hf:
                                 e.tensor_copy(
                                     a[:].rearrange("p (cc t) -> p cc t",
                                                    cc=NCC)[
                                         :, :, hh * 256:(hh + 1) * 256],
                                     b[:].rearrange("p (cc t) -> p cc t",
                                                    cc=NCC)),
                                 inc=s_xfc[xpar])

                        p.wait("gpsimd", s_t1, max(0, s_t1.v - 16))
                        if tq == 0:
                            p.wait("gpsimd", s_tab, TABV)
                        p.op("gpsimd", lambda e, a=sstrip[tq % 2], t=tq:
                             e.tensor_copy(a[:],
                                           rtab[64:128,
                                                t * 512:(t + 1) * 512]),
                             inc=s_strip)

                        for h in range(NH if not (q_nh and tag == "q")
                                       else q_nh):
                            hpar = h % 2
                            hb = h // 8
                            hloc = h % 8
                            p.wait("tensor", s_xfc[0], s_xfc[0].v)
                            p.wait("tensor", s_xfc[1], s_xfc[1].v)
                            p.wait("tensor", s_ropeD, max(0, s_ropeD.v - 3))
                            p.wait("tensor", s_shA, max(0, s_shA.v - 1))
                            p.wait("tensor", s_ropeA, max(0, s_ropeA.v - 1))
                            for cc in range(NCC):
                                last = cc == NCC - 1
                                p.op("tensor",
                                     lambda e, o=psK[hpar], x_=xb[par], c=cc,
                                     hh=h, st=(cc == 0), sp=last:
                                     e.matmul(o[:],
                                              wslab[:, c * D + hh * 128:
                                                    c * D + hh * 128 + 128],
                                              x_[:, c * 512:(c + 1) * 512],
                                              start=st, stop=sp),
                                     inc=s_mm if last else None)
                            mmv = s_mm.v
                            sgv = s_sp[hb].v

                            p.wait("scalar", s_mm, mmv)
                            p.wait("scalar", s_t1, max(0, s_t1.v - 1))
                            p.op("scalar", lambda e, a=tsh[hpar], o=psK[hpar]:
                                 e.copy(a[0:32, :], o[32:64, :]))
                            p.op("scalar", lambda e, a=tsh[hpar], o=psK[hpar]:
                                 e.copy(a[32:64, :], o[0:32, :]), inc=s_shA)
                            p.wait("scalar", s_sp[hb], sgv)
                            p.op("scalar", lambda e, a=stage[hb], o=psK[hpar],
                                 hl=hloc:
                                 e.copy(a[64:128, hl * 512:hl * 512 + 512],
                                        o[64:128, :]), inc=s_ropeA)
                            p.wait("gpsimd", s_shA, s_shA.v)
                            p.wait("gpsimd", s_ropeD, max(0, s_ropeD.v - 2))
                            p.wait("gpsimd", s_strip, s_strip.v)
                            p.op("gpsimd", lambda e, a=tm1[hpar], b=tsh[hpar],
                                 c=sstrip[tq % 2]:
                                 e.tensor_mul(a[:], b[:], c[:]),
                                 inc=s_t1)
                            p.wait("vector", s_mm, mmv)
                            p.wait("vector", s_ropeD, max(0, s_ropeD.v - 2))
                            p.wait("vector", s_shA, s_shA.v)
                            p.wait("vector", s_ropeA, s_ropeA.v)
                            if first_dve_tab:
                                p.wait("vector", s_tab, TABV)
                                first_dve_tab = False
                            p.op("vector", lambda e, a=tm2[hpar], o=psK[hpar],
                                 t=tq:
                                 e.tensor_mul(a[:], o[0:64, :],
                                              rtab[0:64,
                                                   t * 512:(t + 1) * 512]),
                                 inc=s_ropeD)
                            p.wait("vector", s_t1, s_t1.v)
                            p.wait("vector", s_ropeD, s_ropeD.v)
                            p.wait("vector", s_sp[hb], sgv)
                            p.op("vector", lambda e, a=stage[hb], b=tm1[hpar],
                                 c=tm2[hpar], hl=hloc:
                                 e.tensor_add(a[0:64, hl * 512:hl * 512 + 512],
                                              b[:], c[:]), inc=s_ropeD)
                            if h == 7 or h == 15:
                                hbase = h - 7
                                p.wait("scalar", s_ropeD, s_ropeD.v)
                                p.wait("scalar", s_ropeA, s_ropeA.v)
                                sp3 = spill.rearrange("p (h t) -> p h t", h=NH)
                                p.op("scalar",
                                     lambda e, a=stage[hb], b=sp3, t=tq,
                                     hh=hbase:
                                     e.dma_start(
                                         out=b[:, hh:hh + 8,
                                               t * 512:(t + 1) * 512],
                                         in_=a[:].rearrange(
                                             "p (h t) -> p h t", h=8)),
                                     inc=s_sp[hb], n=16)
                        p.op("tensor", lambda e: e.nop(), inc=s_xbf)
                    return [(s_sp[0], s_sp[0].v), (s_sp[1], s_sp[1].v),
                            (s_xfc[0], s_xfc[0].v), (s_xfc[1], s_xfc[1].v)]

            def vproj_phase(prev_fence):
                ntq = T // 512
                with ExitStack() as ph:
                    def sb(name, shape, dt):
                        return ph.enter_context(
                            nc.sbuf_tensor("v" + name, shape, dt))
                    xb = [sb("xb0", [128, NCC * 512], BF16),
                          sb("xb1", [128, NCC * 512], BF16)]
                    vstage = [sb("vs0", [128, D], BF16),
                              sb("vs1", [128, D], BF16)]
                    psV = [ph.enter_context(
                               nc.psum_tensor("vpsV0", [128, D], F32)),
                           ph.enter_context(
                               nc.psum_tensor("vpsV1", [128, D], F32))]
                    s_mm = sem("vs_mm")
                    s_ev = sem("vs_ev")
                    s_sp = [sem("vs_sp0"), sem("vs_sp1")]
                    s_xbf = sem("vs_xbf")

                    x3 = xt.rearrange("(cc p) t -> p cc t", p=128)
                    for tq in range(ntq):
                        par = tq % 2
                        for hf in range(2):
                            xpar = (2 * tq + hf) % 2
                            p.wait("sync", s_xfc[xpar], s_xfc[xpar].v)
                            if tq == 0 and hf == 0:
                                for s, v in prev_fence:
                                    p.wait("sync", s, v)
                            t0 = tq * 512 + hf * 256
                            p.op("sync", lambda e, a=xf[xpar], b=x3, o=t0:
                                 e.dma_start(
                                     out=a[:].rearrange("p (cc t) -> p cc t",
                                                        cc=NCC),
                                     in_=b[:, :, o:o + 256]),
                                 inc=s_xf[xpar], n=16)
                            p.wait("vector", s_xf[xpar], s_xf[xpar].v)
                            p.wait("vector", s_xbf, max(0, s_xbf.v - 1))
                            if tq == 0 and hf == 0:
                                for s, v in prev_fence:
                                    p.wait("vector", s, v)
                            p.op("vector", lambda e, a=xb[par], b=xf[xpar],
                                 hh=hf:
                                 e.tensor_copy(
                                     a[:].rearrange("p (cc t) -> p cc t",
                                                    cc=NCC)[
                                         :, :, hh * 256:(hh + 1) * 256],
                                     b[:].rearrange("p (cc t) -> p cc t",
                                                    cc=NCC)),
                                 inc=s_xfc[xpar])

                        for ks in range(4):
                            vpar = ks % 2
                            kc = tq * 4 + ks
                            p.wait("tensor", s_xfc[0], s_xfc[0].v)
                            p.wait("tensor", s_xfc[1], s_xfc[1].v)
                            p.wait("tensor", s_ev, max(0, s_ev.v - 1))
                            for cc in range(NCC):
                                for nn in range(4):
                                    last = cc == NCC - 1 and nn == 3
                                    p.op("tensor",
                                         lambda e, o=psV[vpar], x_=xb[par],
                                         c=cc, k=ks, n_=nn, st=(cc == 0),
                                         sp=(cc == NCC - 1):
                                         e.matmul(
                                             o[:, n_ * 512:(n_ + 1) * 512],
                                             x_[:, c * 512 + k * 128:
                                                c * 512 + k * 128 + 128],
                                             wslab[:, c * D + n_ * 512:
                                                   c * D + (n_ + 1) * 512],
                                             start=st, stop=sp),
                                         inc=s_mm if last else None)
                            p.wait("vector", s_mm, s_mm.v)
                            p.wait("vector", s_sp[vpar], s_sp[vpar].v)
                            p.op("vector", lambda e, a=vstage[vpar],
                                 o=psV[vpar]:
                                 e.tensor_copy(a[:], o[:]), inc=s_ev)
                            p.wait("scalar", s_ev, s_ev.v)
                            p.op("scalar", lambda e, a=vstage[vpar], k=kc:
                                 e.dma_start(
                                     out=vspill[:, :, k, :],
                                     in_=a[:].rearrange("p (h d) -> p h d",
                                                        h=NH)),
                                 inc=s_sp[vpar], n=16)
                        p.op("tensor", lambda e: e.nop(), inc=s_xbf)
                    return [(s_sp[0], s_sp[0].v), (s_sp[1], s_sp[1].v),
                            (s_xfc[0], s_xfc[0].v), (s_xfc[1], s_xfc[1].v)]

            fw = build_w_slab(wq, wslab, xf, None, wait_first=fence_init)
            if upto == "w":
                for s, v in fw:
                    p.wait("sync", s, v)
                with nc.Block() as block:
                    p.flush(block)
                return nc
            fq = proj_phase("q", xq, TLOC, ropeq, qspill, fw)
            if upto == "q":
                for s, v in fq:
                    p.wait("sync", s, v)
            else:
                fw = build_w_slab(wk, wslab, xf, s_xfc, wait_first=fq)
                fk = proj_phase("k", xt, T, ropek, kspill, fw)
                if upto == "k":
                    for s, v in fk:
                        p.wait("sync", s, v)
                else:
                    fw = build_w_slab(wv, wslab, xf, s_xfc, wait_first=fk)
                    fv = vproj_phase(fw)
            fproj = (fq[:2] + fk[:2] + fv) if upto not in ("q", "k") else []

        if upto in ("q", "k", "proj"):
            if upto == "proj":
                for s, v in fproj:
                    p.wait("sync", s, v)
            with nc.Block() as block:
                p.flush(block)
            if verbose:
                print("instruction queue sizes:", p.counts())
            return nc

        with ExitStack() as ao:
            Yall = ao.enter_context(
                nc.sbuf_tensor("Yall", [128, NH * TLOC], BF16))
            s_yall = sem("s_yall")
            with ExitStack() as ph:
                def sb(name, shape, dt):
                    return ph.enter_context(nc.sbuf_tensor("a" + name, shape, dt))
                kb = [sb("kb0", [128, T], BF16), sb("kb1", [128, T], BF16)]
                vb = [sb("vb0", [128, T], BF16), sb("vb1", [128, T], BF16)]
                qb = [sb("qb0", [128, TLOC], BF16), sb("qb1", [128, TLOC], BF16)]
                at = [sb(f"at{i}", [128, 1024], BF16) for i in range(4)]
                rr = sb("rr", [1, 1024], F32)
                rb = sb("rb", [128, 1024], F32)
                acc = sb("acc", [128, 1024], F32)
                accb = sb("accb", [128, 1024], BF16)
                psS = [ph.enter_context(nc.psum_tensor("apsS0", [128, 1024], F32)),
                       ph.enter_context(nc.psum_tensor("apsS1", [128, 1024], F32))]
                psY = ph.enter_context(nc.psum_tensor("apsY", [128, 1024], F32))
                psAux = ph.enter_context(nc.psum_tensor("apsAux", [128, 1024], F32))

                s_ld = [sem("a_ld0"), sem("a_ld1")]
                s_S = sem("a_S")
                s_exp = sem("a_exp")
                s_AV = sem("a_AV")
                s_acc = sem("a_acc")
                s_accb = sem("a_accb")
                s_sum = sem("a_sum")
                s_rec = sem("a_rec")
                s_bc = sem("a_bc")
                s_rbc = sem("a_rbc")
                s_hd = sem("a_hd")

                for h in range(NH):
                    par = h % 2
                    if h == 0:
                        for s, v in fproj:
                            p.wait("sync", s, v)
                    p.wait("sync", s_hd, max(0, s_hd.v - 1))
                    p.op("sync", lambda e, a=kb[par], hh=h:
                         e.dma_start(out=a[:],
                                     in_=kspill[:, hh * T:(hh + 1) * T]),
                         inc=s_ld[par], n=16)
                    p.op("sync", lambda e, a=vb[par], hh=h:
                         e.dma_start(
                             out=a[:].rearrange("p (kc d) -> p kc d", kc=NKC),
                             in_=vspill[:, hh, :, :]),
                         inc=s_ld[par], n=16)
                    p.op("sync", lambda e, a=qb[par], hh=h:
                         e.dma_start(out=a[:],
                                     in_=qspill[:, hh * TLOC:(hh + 1) * TLOC]),
                         inc=s_ld[par], n=16)
                    ldv = s_ld[par].v

                    for sw in range(2):
                        q0 = sw * 1024

                        def emit_S(kc):
                            spar = kc % 2
                            for half in range(2):
                                p.op("tensor",
                                     lambda e, o=psS[spar], k_=kb[par],
                                     q_=qb[par], c=kc, hh=half, qo=q0:
                                     e.matmul(o[:, hh * 512:(hh + 1) * 512],
                                              k_[:, c * 128:(c + 1) * 128],
                                              q_[:, qo + hh * 512:
                                                 qo + (hh + 1) * 512],
                                              start=True, stop=True),
                                     inc=s_S if half == 1 else None)
                            return s_S.v

                        p.wait("tensor", s_ld[par], ldv)
                        p.wait("tensor", s_yall, s_yall.v)
                        p.wait("tensor", s_rbc, s_rbc.v)
                        p.wait("tensor", s_exp, s_exp.v)
                        sval = [emit_S(0)]
                        accv = {}
                        for kc in range(NKC):
                            spar = kc % 2
                            if kc + 1 < NKC:
                                p.wait("tensor", s_exp, s_exp.v)
                                sval.append(emit_S(kc + 1))
                            if kc >= 1 and (kc - 1) % 2 == 0:
                                for half in range(2):
                                    p.op("tensor",
                                         lambda e, a=at[(kc - 1) % 4],
                                         hh=half, st=(kc == 1):
                                         e.matmul(
                                             psAux[0:1,
                                                   hh * 512:(hh + 1) * 512],
                                             onec_t[:],
                                             a[:, hh * 512:(hh + 1) * 512],
                                             start=st, stop=False),
                                         inc=s_AV if half == 1 else None)
                            apar = kc % 4
                            p.wait("scalar", s_S, sval[kc])
                            if kc >= 4 and (kc - 4) % 2 == 1:
                                p.wait("scalar", s_acc, accv[kc - 4])
                            p.op("scalar", lambda e, a=at[apar], o=psS[spar]:
                                 e.activation(a[:], o[:], AF.Exp,
                                              bias=c_expbias, scale=SCL),
                                 inc=s_exp)
                            p.wait("tensor", s_exp, s_exp.v)
                            for half in range(2):
                                p.op("tensor",
                                     lambda e, v_=vb[par], a=at[apar], c=kc,
                                     hh=half:
                                     e.matmul(psY[:, hh * 512:(hh + 1) * 512],
                                              v_[:, c * 128:(c + 1) * 128],
                                              a[:, hh * 512:(hh + 1) * 512],
                                              start=(c == 0),
                                              stop=(c == NKC - 1)),
                                     inc=s_AV if half == 1 else None)
                            if kc % 2 == 1:
                                p.wait("vector", s_exp, s_exp.v)
                                p.wait("vector", s_acc, s_acc.v)
                                if kc == 1:
                                    p.wait("vector", s_accb, s_accb.v)
                                    p.op("vector", lambda e, a=at[apar]:
                                         e.tensor_copy(acc[:], a[:]),
                                         inc=s_acc)
                                else:
                                    p.op("vector", lambda e, a=at[apar]:
                                         e.tensor_add(acc[:], acc[:], a[:]),
                                         inc=s_acc)
                            accv[kc] = s_acc.v
                        p.wait("vector", s_acc, s_acc.v)
                        p.wait("vector", s_sum, s_sum.v)
                        p.op("vector", lambda e: e.tensor_copy(accb[:], acc[:]),
                             inc=s_accb)
                        p.wait("tensor", s_accb, s_accb.v)
                        for half in range(2):
                            p.op("tensor", lambda e, hh=half:
                                 e.matmul(psAux[0:1, hh * 512:(hh + 1) * 512],
                                          onec_t[:],
                                          accb[:, hh * 512:(hh + 1) * 512],
                                          start=False, stop=True),
                                 inc=s_sum if half == 1 else None)
                        p.wait("vector", s_sum, s_sum.v)
                        p.wait("vector", s_bc, s_bc.v)
                        p.op("vector", lambda e: e.reciprocal(rr[:],
                                                              psAux[0:1, :]),
                             inc=s_rec)
                        p.wait("tensor", s_rec, s_rec.v)
                        for half in range(2):
                            p.op("tensor", lambda e, hh=half:
                                 e.matmul(psAux[:, hh * 512:(hh + 1) * 512],
                                          oner_t[:],
                                          rr[0:1, hh * 512:(hh + 1) * 512],
                                          start=True, stop=True),
                                 inc=s_bc if half == 1 else None)
                        p.wait("vector", s_bc, s_bc.v)
                        p.wait("vector", s_yall, s_yall.v)
                        p.op("vector", lambda e: e.tensor_copy(rb[:], psAux[:]),
                             inc=s_rbc)
                        p.wait("vector", s_AV, s_AV.v)
                        p.wait("vector", s_rbc, s_rbc.v)
                        p.op("vector", lambda e, hh=h, qo=q0:
                             e.tensor_mul(
                                 Yall[:, hh * TLOC + qo:hh * TLOC + qo + 1024],
                                 psY[:], rb[:]),
                             inc=s_yall)
                    p.op("tensor", lambda e: e.nop(), inc=s_hd)
                fatt = [(s_hd, s_hd.v), (s_yall, s_yall.v)]
                if upto == "att":
                    p.wait("sync", s_hd, s_hd.v)
                    with nc.Block() as block:
                        p.flush(block)
                    if verbose:
                        print("instruction queue sizes:", p.counts())
                    return nc

            def ln_rows(psMu, psM2, rows, rstdr, psBc, abslab, S, abc_guards):
                s_stats, s_rows, s_rstd, s_bcmm, s_abc = S
                p.wait("vector", s_stats, s_stats.v)
                p.op("vector", lambda e: e.tensor_scalar_mul(
                    rows[0:1, 0:512], psMu[0:1, 0:512], 1.0 / D), inc=s_rows)
                p.op("vector", lambda e: e.tensor_scalar_mul(
                    rows[0:1, 512:1024], psM2[0:1, 0:512], 1.0 / D), inc=s_rows)
                p.wait("vector", s_rows, s_rows.v)
                p.op("vector", lambda e: e.tensor_mul(
                    rows[0:1, 1024:1536], rows[0:1, 0:512], rows[0:1, 0:512]),
                    inc=s_rows)
                p.wait("vector", s_rows, s_rows.v)
                p.op("vector", lambda e: e.tensor_sub(
                    rows[0:1, 1536:2048], rows[0:1, 512:1024],
                    rows[0:1, 1024:1536]), inc=s_rows)
                p.wait("scalar", s_rows, s_rows.v)
                p.op("scalar", lambda e: e.activation(
                    rstdr[0:1, 0:512], rows[0:1, 1536:2048], AF.Ln,
                    bias=c_eps), inc=s_rstd)
                p.wait("scalar", s_rstd, s_rstd.v)
                p.op("scalar", lambda e: e.activation(
                    rstdr[0:1, 512:1024], rstdr[0:1, 0:512], AF.Exp,
                    scale=-0.5), inc=s_rstd)
                p.wait("vector", s_rstd, s_rstd.v)
                p.op("vector", lambda e: e.scalar_tensor_tensor(
                    rstdr[0:1, 1024:1536], rows[0:1, 0:512], -1.0,
                    rstdr[0:1, 512:1024], ALU.mult, ALU.mult), inc=s_rows)
                p.wait("tensor", s_rows, s_rows.v)
                p.wait("tensor", s_rstd, s_rstd.v)
                p.wait("tensor", s_abc, s_abc.v)
                p.op("tensor", lambda e: e.matmul(
                    psBc[:, 0:512], oner_t[:], rstdr[0:1, 512:1024],
                    start=True, stop=True), inc=None)
                p.op("tensor", lambda e: e.matmul(
                    psBc[:, 512:1024], oner_t[:], rstdr[0:1, 1024:1536],
                    start=True, stop=True), inc=s_bcmm)
                p.wait("vector", s_bcmm, s_bcmm.v)
                for s, v in abc_guards:
                    p.wait("vector", s, v)
                p.op("vector", lambda e: e.tensor_copy(abslab[:], psBc[:]),
                     inc=s_abc)

            with ExitStack() as ph:
                def sb(name, shape, dt):
                    return ph.enter_context(nc.sbuf_tensor("o" + name, shape, dt))
                wslab2 = sb("wslab2", [128, NCC * D], BF16)
                with ExitStack() as stgsc:
                    stg2 = [stgsc.enter_context(
                                nc.sbuf_tensor("ostg0", [128, 4096], F32)),
                            stgsc.enter_context(
                                nc.sbuf_tensor("ostg1", [128, 4096], F32))]
                    fw = build_w_slab(wo, wslab2, stg2, None,
                                      wait_first=fatt)
                WOC = fw[0][1]
                sslab = sb("sslab", [128, NCC * 512], F32)
                x2b16 = sb("x2b16", [128, NCC * 512], BF16)
                sb16 = [sb("sb16_0", [128, 512], BF16),
                        sb("sb16_1", [128, 512], BF16)]
                sqb = [sb("sqb0", [128, 512], BF16), sb("sqb1", [128, 512], BF16)]
                xres = [sb("xres0", [128, 512], F32), sb("xres1", [128, 512], F32)]
                rows = sb("rows", [1, 2048], F32)
                rstdr = sb("rstdr", [1, 1536], F32)
                abslab = sb("abslab", [128, 1024], F32)
                psO = [ph.enter_context(nc.psum_tensor("opsO0", [128, 512], F32)),
                       ph.enter_context(nc.psum_tensor("opsO1", [128, 512], F32))]
                psMu = ph.enter_context(nc.psum_tensor("opsMu", [128, 512], F32))
                psM2 = ph.enter_context(nc.psum_tensor("opsM2", [128, 512], F32))
                psBc = ph.enter_context(nc.psum_tensor("opsBc", [128, 1024], F32))

                s_xr = [sem("o_xr0"), sem("o_xr1")]
                s_mm = sem("o_mm")
                s_ev = sem("o_ev")
                s_c16 = sem("o_c16")
                s_sq = sem("o_sq")
                s_stats = sem("o_stats")
                s_rows = sem("o_rows")
                s_rstd = sem("o_rstd")
                s_bcmm = sem("o_bcmm")
                s_abc = sem("o_abc")
                s_x2 = sem("o_x2")
                s_x2c = sem("o_x2c")
                s_spf = sem("o_spf")
                s_spb = sem("o_spb")

                xq3 = xq.rearrange("(cc p) t -> p cc t", p=128)
                for tc in range(4):
                    for cc in range(NCC):
                        par = cc % 2
                        p.wait("sync", s_ev, max(0, s_ev.v - 1))
                        if tc == 0 and cc == 0:
                            for s, v in fw:
                                p.wait("sync", s, v)
                        p.op("sync", lambda e, a=xres[par], c=cc, t=tc:
                             e.dma_start(out=a[:],
                                         in_=xq3[:, c, t * 512:(t + 1) * 512]),
                             inc=s_xr[par], n=16)
                        if tc == 0 and cc == 0:
                            p.wait("tensor", s_wc, WOC)
                            p.wait("tensor", s_yall, s_yall.v)
                        p.wait("tensor", s_ev, max(0, s_ev.v - 1))
                        for h in range(NH):
                            p.op("tensor",
                                 lambda e, o=psO[par], hh=h, c=cc, t=tc,
                                 st=(h == 0), sp=(h == NH - 1):
                                 e.matmul(o[:],
                                          wslab2[:, hh * D + c * 128:
                                                 hh * D + c * 128 + 128],
                                          Yall[:, hh * TLOC + t * 512:
                                               hh * TLOC + (t + 1) * 512],
                                          start=st, stop=sp),
                                 inc=s_mm if h == NH - 1 else None)
                        p.wait("vector", s_mm, s_mm.v)
                        p.wait("vector", s_xr[par], s_xr[par].v)
                        if tc > 0 and cc == 0:
                            p.wait("vector", s_spf, s_spf.v)
                            p.wait("vector", s_spb, s_spb.v)
                        p.op("vector", lambda e, c=cc, par_=par:
                             e.scalar_tensor_tensor(
                                 sslab[:, c * 512:(c + 1) * 512], psO[par_][:],
                                 1.0, xres[par_][:], ALU.bypass, ALU.add),
                             inc=s_ev)
                        p.wait("scalar", s_ev, s_ev.v)
                        p.wait("scalar", s_stats, max(0, s_stats.v - 1))
                        p.op("scalar", lambda e, c=cc, par_=par:
                             e.copy(sb16[par_][:],
                                    sslab[:, c * 512:(c + 1) * 512]),
                             inc=s_c16)
                        p.wait("vector", s_c16, s_c16.v)
                        p.wait("vector", s_stats, max(0, s_stats.v - 1))
                        p.op("vector", lambda e, par_=par:
                             e.tensor_mul(sqb[par_][:], sb16[par_][:],
                                          sb16[par_][:]), inc=s_sq)
                        p.wait("tensor", s_c16, s_c16.v)
                        p.op("tensor", lambda e, c=cc, par_=par:
                             e.matmul(psMu[0:1, :], onec_t[:], sb16[par_][:],
                                      start=(c == 0), stop=(c == NCC - 1)),
                             inc=None)
                        p.wait("tensor", s_sq, s_sq.v)
                        p.op("tensor", lambda e, c=cc, par_=par:
                             e.matmul(psM2[0:1, :], onec_t[:], sqb[par_][:],
                                      start=(c == 0), stop=(c == NCC - 1)),
                             inc=s_stats)
                    ln_rows(psMu, psM2, rows, rstdr, psBc, abslab,
                            (s_stats, s_rows, s_rstd, s_bcmm, s_abc),
                            [(s_x2, s_x2.v)])
                    p.wait("vector", s_abc, s_abc.v)
                    for cc in range(NCC):
                        p.op("vector", lambda e, c=cc:
                             e.tensor_mul(sslab[:, c * 512:(c + 1) * 512],
                                          sslab[:, c * 512:(c + 1) * 512],
                                          abslab[:, 0:512]), inc=s_x2)
                        p.wait("vector", s_x2, s_x2.v)
                        p.op("vector", lambda e, c=cc:
                             e.tensor_add(sslab[:, c * 512:(c + 1) * 512],
                                          sslab[:, c * 512:(c + 1) * 512],
                                          abslab[:, 512:1024]), inc=s_x2)
                        p.wait("scalar", s_x2, s_x2.v)
                        p.op("scalar", lambda e, c=cc:
                             e.copy(x2b16[:, c * 512:(c + 1) * 512],
                                    sslab[:, c * 512:(c + 1) * 512]),
                             inc=s_x2c)
                    p.wait("sync", s_x2, s_x2.v)
                    p.op("sync", lambda e, t=tc:
                         e.dma_start(
                             out=x2spill[:, :, t * 512:(t + 1) * 512],
                             in_=sslab[:].rearrange("p (cc t) -> p cc t",
                                                    cc=NCC)),
                         inc=s_spf, n=16)
                    p.wait("sync", s_x2c, s_x2c.v)
                    p.op("sync", lambda e, t=tc:
                         e.dma_start(
                             out=x2bspill[:, :, t * 512:(t + 1) * 512],
                             in_=x2b16[:].rearrange("p (cc t) -> p cc t",
                                                    cc=NCC)),
                         inc=s_spb, n=16)
                fout = [(s_spf, s_spf.v), (s_spb, s_spb.v)]
                if upto == "out":
                    for s, v in fout:
                        p.wait("sync", s, v)
                    with nc.Block() as block:
                        p.flush(block)
                    if verbose:
                        print("instruction queue sizes:", p.counts())
                    return nc

        s_yst = sem("s_yst")
        NIW = 256
        NIU = II // NIW
        with ExitStack() as ph:
            def sb(name, shape, dt):
                return ph.enter_context(nc.sbuf_tensor("f" + name, shape, dt))
            hsl = sb("hsl", [128, NGI * 512], BF16)
            x2b = sb("x2b", [128, NCC * 512], BF16)
            wg8 = [sb("wg80", [128, NCC * NIW], U8),
                   sb("wg81", [128, NCC * NIW], U8)]
            wu8 = [sb("wu80", [128, NCC * NIW], U8),
                   sb("wu81", [128, NCC * NIW], U8)]
            wgb = [sb("wgb0", [128, NCC * NIW], BF16),
                   sb("wgb1", [128, NCC * NIW], BF16)]
            wub = [sb("wub0", [128, NCC * NIW], BF16),
                   sb("wub1", [128, NCC * NIW], BF16)]
            sgb = [sb("sgb0", [128, 512], BF16), sb("sgb1", [128, 512], BF16)]
            sub = [sb("sub0", [128, 512], BF16), sb("sub1", [128, 512], BF16)]
            wd8 = [sb(f"wd8{i}", [128, 4 * 512], U8) for i in range(2)]
            wdb = [sb(f"wdb{i}", [128, 4 * 512], BF16) for i in range(2)]
            f_sslab = sb("f_sslab", [128, NCC * 512], F32)
            f_sb16 = [sb("sb16_0", [128, 512], BF16), sb("sb16_1", [128, 512], BF16)]
            f_sqb = [sb("sqb0", [128, 512], BF16), sb("sqb1", [128, 512], BF16)]
            f_xres = [sb("xres0", [128, 512], F32), sb("xres1", [128, 512], F32)]
            f_rows = sb("f_rows", [1, 2048], F32)
            f_rstdr = sb("f_rstdr", [1, 1536], F32)
            f_abslab = sb("f_abslab", [128, 1024], F32)

            s_x2l = sem("f_x2l")
            s_g8 = [sem("f_g80"), sem("f_g81")]
            s_u8 = [sem("f_u80"), sem("f_u81")]
            s_gc = [sem("f_gc0"), sem("f_gc1")]
            s_uc = [sem("f_uc0"), sem("f_uc1")]
            s_gu = sem("f_gu")
            s_sil = sem("f_sil")
            s_usc = sem("f_usc")
            s_h = sem("f_h")
            s_d8 = [sem(f"f_d8{i}") for i in range(2)]
            s_dc = [sem(f"f_dc{i}") for i in range(2)]
            s_dgi = sem("f_dgi")
            f_s_ev = sem("f_ev")
            f_s_xr = [sem("f_xr0"), sem("f_xr1")]
            f_s_c16 = sem("f_c16")
            f_s_sq = sem("f_sq")
            f_s_stats = sem("f_stats")
            f_s_rows = sem("f_rows")
            f_s_rstd = sem("f_rstd")
            f_s_bcmm = sem("f_bcmm")
            f_s_abc = sem("f_abc")
            s_x2ap = sem("f_x2ap")

            wg3 = wg.rearrange("(cc p) i -> p cc i", p=128)
            wu3 = wu.rearrange("(cc p) i -> p cc i", p=128)
            NIL = NIW // 128

            for tc in range(4):
                with ExitStack() as gsc:
                    psG = [gsc.enter_context(
                               nc.psum_tensor(f"fpsG{tc}_0", [128, 512], F32)),
                           gsc.enter_context(
                               nc.psum_tensor(f"fpsG{tc}_1", [128, 512], F32))]
                    psU = [gsc.enter_context(
                               nc.psum_tensor(f"fpsU{tc}_0", [128, 512], F32)),
                           gsc.enter_context(
                               nc.psum_tensor(f"fpsU{tc}_1", [128, 512], F32))]

                    p.wait("sync", s_gu, s_gu.v)
                    if tc == 0:
                        for s, v in fout:
                            p.wait("sync", s, v)
                    p.op("sync", lambda e, t=tc:
                         e.dma_start(
                             out=x2b[:].rearrange("p (cc t) -> p cc t", cc=NCC),
                             in_=x2bspill[:, :, t * 512:(t + 1) * 512]),
                         inc=s_x2l, n=16)
                    def emit_gu_prefetch(iu):
                        par = iu % 2
                        i0 = iu * NIW
                        guard = max(0, s_gu.v - (NIL if iu == 0 else 0))
                        p.wait("sync", s_gc[par], s_gc[par].v)
                        p.op("sync", lambda e, a=wg8[par], o=i0:
                             e.dma_start(
                                 out=a[:].rearrange("p (cc i) -> p cc i",
                                                    cc=NCC),
                                 in_=wg3[:, :, o:o + NIW]),
                             inc=s_g8[par], n=16)
                        p.wait("sync", s_uc[par], s_uc[par].v)
                        p.op("sync", lambda e, a=wu8[par], o=i0:
                             e.dma_start(
                                 out=a[:].rearrange("p (cc i) -> p cc i",
                                                    cc=NCC),
                                 in_=wu3[:, :, o:o + NIW]),
                             inc=s_u8[par], n=16)
                        p.wait("vector", s_g8[par], s_g8[par].v)
                        p.wait("vector", s_gu, guard)
                        p.op("vector", lambda e, a=wgb[par], b=wg8[par]:
                             e.tensor_scalar(a[:], b[:], -128.0, None,
                                             ALU.add), inc=s_gc[par])
                        p.wait("scalar", s_u8[par], s_u8[par].v)
                        p.wait("scalar", s_gu, guard)
                        p.op("scalar", lambda e, a=wub[par], b=wu8[par]:
                             e.activation(a[:], b[:], AF.Copy, bias=-128.0),
                             inc=s_uc[par])

                    emit_gu_prefetch(0)
                    for iu in range(NIU):
                        par = iu % 2
                        if iu + 1 < NIU:
                            emit_gu_prefetch(iu + 1)
                        for il in range(NIL):
                            gi = iu * NIL + il
                            gpar = gi % 2
                            p.wait("tensor", s_gc[par], s_gc[par].v)
                            p.wait("tensor", s_uc[par], s_uc[par].v)
                            if gi == 0:
                                p.wait("tensor", s_x2l, s_x2l.v)
                                if tc > 0:
                                    p.wait("tensor", f_s_ev, f_s_ev.v)
                                    p.wait("tensor", f_s_abc, f_s_abc.v)
                            p.wait("tensor", s_sil, max(0, s_sil.v - 1))
                            p.wait("tensor", s_usc, max(0, s_usc.v - 1))
                            for cc in range(NCC):
                                p.op("tensor",
                                     lambda e, o=psG[gpar], w=wgb[par], c=cc,
                                     i=il, st=(cc == 0), sp=(cc == NCC - 1):
                                     e.matmul(o[:],
                                              w[:, c * NIW + i * 128:
                                                c * NIW + i * 128 + 128],
                                              x2b[:, c * 512:(c + 1) * 512],
                                              start=st, stop=sp),
                                     inc=None)
                            for cc in range(NCC):
                                p.op("tensor",
                                     lambda e, o=psU[gpar], w=wub[par], c=cc,
                                     i=il, st=(cc == 0), sp=(cc == NCC - 1):
                                     e.matmul(o[:],
                                              w[:, c * NIW + i * 128:
                                                c * NIW + i * 128 + 128],
                                              x2b[:, c * 512:(c + 1) * 512],
                                              start=st, stop=sp),
                                     inc=s_gu if cc == NCC - 1 else None)
                            p.wait("scalar", s_gu, s_gu.v)
                            p.wait("scalar", s_h, max(0, s_h.v - 1))
                            p.op("scalar", lambda e, o=psG[gpar], g=gi,
                                 par_=gpar:
                                 e.activation(sgb[par_][:], o[:], AF.Silu,
                                              scale=gs_t[:, g:g + 1]),
                                 inc=s_sil)
                            p.op("scalar", lambda e, o=psU[gpar], g=gi,
                                 par_=gpar:
                                 e.activation(sub[par_][:], o[:], AF.Copy,
                                              scale=us_t[:, g:g + 1]),
                                 inc=s_usc)
                            p.wait("vector", s_sil, s_sil.v)
                            p.wait("vector", s_usc, s_usc.v)
                            if tc > 0 and gi == 0:
                                p.wait("vector", s_dgi, s_dgi.v)
                            p.op("vector", lambda e, g=gi, par_=gpar:
                                 e.tensor_mul(hsl[:, g * 512:(g + 1) * 512],
                                              sgb[par_][:], sub[par_][:]),
                                 inc=s_h)
                    HC = s_h.v

                with ExitStack() as dsc:
                    psD = dsc.enter_context(
                        nc.psum_tensor(f"fpsD{tc}", [128, 2048], F32))
                    f_psMu = dsc.enter_context(
                        nc.psum_tensor(f"fpsMu{tc}", [128, 512], F32))
                    f_psM2 = dsc.enter_context(
                        nc.psum_tensor(f"fpsM2{tc}", [128, 512], F32))
                    f_psBc = dsc.enter_context(
                        nc.psum_tensor(f"fpsBc{tc}", [128, 1024], F32))

                    for cg in range(4):
                        for gb in range(NGI // 4):
                            bpar = gb % 2
                            ring = "sync" if gb % 2 == 0 else "scalar"
                            p.wait(ring, s_dc[bpar], s_dc[bpar].v)
                            p.op(ring, lambda e, a=wd8[bpar], g=gb, c=cg:
                                 e.dma_start(
                                     out=a[:].rearrange(
                                         "p (gs c) -> p gs c", gs=4),
                                     in_=wd[g * 512:(g + 1) * 512,
                                            c * 512:(c + 1) * 512].rearrange(
                                         "(gs p) c -> p gs c", p=128)),
                                 inc=s_d8[bpar], n=16)
                            p.wait("scalar", s_d8[bpar], s_d8[bpar].v)
                            p.wait("scalar", s_dgi, max(0, s_dgi.v - 4))
                            p.op("scalar", lambda e, a=wdb[bpar], b=wd8[bpar]:
                                 e.activation(a[:], b[:], AF.Copy,
                                              bias=-128.0), inc=s_dc[bpar])
                            p.wait("tensor", s_dc[bpar], s_dc[bpar].v)
                            if gb == 0:
                                p.wait("tensor", s_h, HC)
                                p.wait("tensor", f_s_ev, f_s_ev.v)
                            for gs in range(4):
                                gi = gb * 4 + gs
                                glast = gi == NGI - 1
                                for cl in range(4):
                                    p.op("tensor",
                                         lambda e, ps=psD, w=wdb[bpar], c=cl,
                                         g=gi, g2=gs, st=(gi == 0), sp=glast:
                                         e.matmul(
                                             ps[:, c * 512:(c + 1) * 512],
                                             w[:, g2 * 512 + c * 128:
                                               g2 * 512 + c * 128 + 128],
                                             hsl[:, g * 512:(g + 1) * 512],
                                             start=st, stop=sp),
                                         inc=s_dgi if cl == 3 else None)
                        for cl in range(4):
                            cc = cg * 4 + cl
                            par = cc % 2
                            p.wait("sync", f_s_ev, max(0, f_s_ev.v - 1))
                            p.op("sync", lambda e, a=f_xres[par], c=cc, t=tc:
                                 e.dma_start(
                                     out=a[:],
                                     in_=x2spill[:, c,
                                                 t * 512:(t + 1) * 512]),
                                 inc=f_s_xr[par], n=16)
                            p.wait("vector", s_dgi, s_dgi.v)
                            p.wait("vector", f_s_xr[par], f_s_xr[par].v)
                            if tc > 0 and cc == 0:
                                p.wait("vector", s_yst, s_yst.v)
                            p.op("vector",
                                 lambda e, ps=psD, c=cc, cl_=cl, par_=par:
                                 e.scalar_tensor_tensor(
                                     f_sslab[:, c * 512:(c + 1) * 512],
                                     ps[:, cl_ * 512:(cl_ + 1) * 512],
                                     ds_t[:, c:c + 1], f_xres[par_][:],
                                     ALU.mult, ALU.add), inc=f_s_ev)
                            p.wait("scalar", f_s_ev, f_s_ev.v)
                            p.wait("scalar", f_s_stats, max(0, f_s_stats.v - 1))
                            p.op("scalar", lambda e, c=cc, par_=par:
                                 e.copy(f_sb16[par_][:],
                                        f_sslab[:, c * 512:(c + 1) * 512]),
                                 inc=f_s_c16)
                            p.wait("vector", f_s_c16, f_s_c16.v)
                            p.wait("vector", f_s_stats, max(0, f_s_stats.v - 1))
                            p.op("vector", lambda e, par_=par:
                                 e.tensor_mul(f_sqb[par_][:], f_sb16[par_][:],
                                              f_sb16[par_][:]), inc=f_s_sq)
                            p.wait("tensor", f_s_c16, f_s_c16.v)
                            p.op("tensor", lambda e, ps=f_psMu, c=cc, par_=par:
                                 e.matmul(ps[0:1, :], onec_t[:],
                                          f_sb16[par_][:], start=(c == 0),
                                          stop=(c == NCC - 1)), inc=None)
                            p.wait("tensor", f_s_sq, f_s_sq.v)
                            p.op("tensor", lambda e, ps=f_psM2, c=cc, par_=par:
                                 e.matmul(ps[0:1, :], onec_t[:],
                                          f_sqb[par_][:], start=(c == 0),
                                          stop=(c == NCC - 1)), inc=f_s_stats)
                    ln_rows(f_psMu, f_psM2, f_rows, f_rstdr, f_psBc, f_abslab,
                            (f_s_stats, f_s_rows, f_s_rstd, f_s_bcmm, f_s_abc),
                            [(s_x2ap, s_x2ap.v)])
                    p.wait("vector", f_s_abc, f_s_abc.v)
                    for cc in range(NCC):
                        p.op("vector", lambda e, c=cc:
                             e.tensor_mul(f_sslab[:, c * 512:(c + 1) * 512],
                                          f_sslab[:, c * 512:(c + 1) * 512],
                                          f_abslab[:, 0:512]), inc=s_x2ap)
                        p.wait("vector", s_x2ap, s_x2ap.v)
                        p.op("vector", lambda e, c=cc:
                             e.tensor_add(f_sslab[:, c * 512:(c + 1) * 512],
                                          f_sslab[:, c * 512:(c + 1) * 512],
                                          f_abslab[:, 512:1024]), inc=s_x2ap)
                    p.wait("sync", s_x2ap, s_x2ap.v)
                    p.op("sync", lambda e, t=tc:
                         e.dma_start(
                             out=y[:, :, t * 512:(t + 1) * 512],
                             in_=f_sslab[:].rearrange("p (cc t) -> p cc t",
                                                    cc=NCC)),
                         inc=s_yst, n=16)
            p.wait("sync", s_yst, s_yst.v)

        with nc.Block() as block:
            p.flush(block)

    if verbose:
        print("instruction queue sizes:", p.counts())
    return nc



def _u8_of_i32(a):
    return a.view(np.uint8).reshape(*a.shape, 4)[..., 0]


def stage_inputs(inputs):
    x = np.asarray(inputs["L0_attn_x"])
    wqh = np.ascontiguousarray(np.asarray(inputs["L0_attn_wq"]).T)
    wkh = np.ascontiguousarray(np.asarray(inputs["L0_attn_wk"]).T)
    wvh = np.ascontiguousarray(np.asarray(inputs["L0_attn_wv"]).T)
    woh = np.ascontiguousarray(np.asarray(inputs["L0_attn_wo"]).T)
    wgT = np.ascontiguousarray(_u8_of_i32(np.asarray(inputs["L0_ffn_gate_w"])).T)
    wuT = np.ascontiguousarray(_u8_of_i32(np.asarray(inputs["L0_ffn_up_w"])).T)
    wdT = np.ascontiguousarray(_u8_of_i32(np.asarray(inputs["L0_ffn_down_w"])).T)
    gsc = np.ascontiguousarray(
        np.asarray(inputs["L0_ffn_gate_scale"]).reshape(NGI, 128).T)
    usc = np.ascontiguousarray(
        np.asarray(inputs["L0_ffn_up_scale"]).reshape(NGI, 128).T)
    dsc = np.ascontiguousarray(
        np.asarray(inputs["L0_ffn_down_scale"]).reshape(NCC, 128).T)

    j = np.arange(64, dtype=np.float64)
    inv = 10000.0 ** (-2.0 * j / 128.0)
    tk = np.arange(T, dtype=np.float64)
    ang = inv[:, None] * tk[None, :]
    ropek_np = np.empty((128, T), dtype=np.float32)
    ropek_np[0:64] = np.cos(ang)
    sn = np.sin(ang)
    sn[:32] *= -1.0
    ropek_np[64:128] = sn

    con = np.zeros((128, 4), dtype=np.float32)
    con[:, 0] = EXP_BIAS
    con[:, 1] = -128.0
    con[:, 2] = EPS
    onec_np = np.ones((128, 1), dtype=ml_dtypes.bfloat16)
    oner_np = np.ones((1, 128), dtype=np.float32)

    xts = [np.ascontiguousarray(x[b].T) for b in range(B)]
    in_maps = []
    for c in range(8):
        b, qb = c // 4, c % 4
        q0 = qb * TLOC
        in_maps.append({
            "xt": xts[b],
            "xq": np.ascontiguousarray(xts[b][:, q0:q0 + TLOC]),
            "wq": wqh, "wk": wkh, "wv": wvh, "wo": woh,
            "wg": wgT, "wu": wuT, "wd": wdT,
            "gs": gsc, "us": usc, "ds": dsc,
            "ropek": ropek_np,
            "ropeq": np.ascontiguousarray(ropek_np[:, q0:q0 + TLOC]),
            "consts": con, "onec": onec_np, "oner": oner_np,
        })
    return in_maps


def _tune_compiler_flags():
    try:
        import libneuronxla.libncc as ncc
        ncc.NEURON_CC_FLAGS = [
            f
            for f in ncc.NEURON_CC_FLAGS
        ]
    except Exception:
        pass


def kernel(**inputs):
    _tune_compiler_flags()
    in_maps = stage_inputs(inputs)
    nc = build_program()
    res = run_bass_kernel_spmd(nc, in_maps, core_ids=list(range(8)))
    out = np.empty((B, T, D), dtype=np.float32)
    for c in range(8):
        b, qb = c // 4, c % 4
        yv = np.asarray(res.results[c]["y"])
        blk = yv.transpose(1, 0, 2).reshape(D, TLOC)
        out[b, qb * TLOC:(qb + 1) * TLOC, :] = blk.T
    return out


if __name__ == "__main__":
    nc = build_program(verbose=True)
    n = sum(len(bb.instructions) for f in nc.m.functions for bb in f.blocks)
    print("total instructions:", n)


# version 15 (final)
# speedup vs baseline: 1.0831x; 1.0831x over previous
import math
from contextlib import ExitStack

import numpy as np
import ml_dtypes

import concourse.bass as bass
import concourse.mybir as mybir
from concourse.bass_utils import run_bass_kernel_spmd

F32 = mybir.dt.float32
BF16 = mybir.dt.bfloat16
U8 = mybir.dt.uint8
AF = mybir.ActivationFunctionType
ALU = mybir.AluOpType

B, T, D, NH, HD, II = 2, 8192, 2048, 16, 128, 8192
TLOC = 2048
NCC = D // 128
NKC = T // 128
NGI = II // 128
SCL = 1.0 / math.sqrt(HD)
EXP_BIAS = -5.0
EPS = 1e-5

ENGS = ("sync", "tensor", "vector", "scalar", "gpsimd")


class SemT:
    def __init__(self, ctx, nc, name):
        self.h = ctx.enter_context(nc.semaphore(name))
        self.v = 0

    def bump(self, n=1):
        self.v += n
        return self.v


class Prog:
    def __init__(self):
        self.q = {e: [] for e in ENGS}

    def emit(self, eng, fn):
        self.q[eng].append(fn)

    def wait(self, eng, sem, val):
        if val > 0:
            self.emit(eng, lambda e, s=sem.h, v=val: e.wait_ge(s, v))

    def op(self, eng, fn, inc=None, n=1):
        if inc is None:
            self.emit(eng, fn)
            return None
        tgt = inc.bump(n)
        self.emit(eng, lambda e, f=fn, s=inc.h, m=n: f(e).then_inc(s, m))
        return tgt

    def flush(self, block):
        for eng in ENGS:
            ops = self.q[eng]
            if not ops:
                continue
            dec = getattr(block, eng)

            def section(e, _ops=ops):
                for f in _ops:
                    f(e)
            dec(section)

    def counts(self):
        return {e: len(self.q[e]) for e in ENGS}


def build_program(verbose=False, upto="full", q_ntq=None, q_nh=None):
    nc = bass.Bass("TRN2", target_bir_lowering=False)

    def dram(name, shape, dt, kind):
        return nc.dram_tensor(name, shape, dt, kind=kind).ap()

    xt = dram("xt", [D, T], F32, "ExternalInput")
    xq = dram("xq", [D, TLOC], F32, "ExternalInput")
    wq = dram("wq", [D, D], F32, "ExternalInput")
    wk = dram("wk", [D, D], F32, "ExternalInput")
    wv = dram("wv", [D, D], F32, "ExternalInput")
    wo = dram("wo", [D, D], F32, "ExternalInput")
    wg = dram("wg", [D, II], U8, "ExternalInput")
    wu = dram("wu", [D, II], U8, "ExternalInput")
    wd = dram("wd", [II, D], U8, "ExternalInput")
    gs = dram("gs", [128, NGI], F32, "ExternalInput")
    us = dram("us", [128, NGI], F32, "ExternalInput")
    ds = dram("ds", [128, NCC], F32, "ExternalInput")
    ropek = dram("ropek", [128, T], F32, "ExternalInput")
    ropeq = dram("ropeq", [128, TLOC], F32, "ExternalInput")
    consts = dram("consts", [128, 4], F32, "ExternalInput")
    onec = dram("onec", [128, 1], BF16, "ExternalInput")
    oner = dram("oner", [1, 128], F32, "ExternalInput")

    kspill = dram("kspill", [128, NH * T], BF16, "Internal")
    vspill = dram("vspill", [128, NH, NKC, 128], BF16, "Internal")
    qspill = dram("qspill", [128, NH * TLOC], BF16, "Internal")
    x2spill = dram("x2spill", [128, NCC, TLOC], F32, "Internal")
    x2bspill = dram("x2bspill", [128, NCC, TLOC], BF16, "Internal")
    y = dram("y", [128, NCC, TLOC], F32, "ExternalOutput")

    p = Prog()

    with ExitStack() as top:
        def sem(name):
            return SemT(top, nc, name)

        def g_sb(name, shape, dt):
            return top.enter_context(nc.sbuf_tensor(name, shape, dt))

        consts_t = g_sb("consts_t", [128, 4], F32)
        onec_t = g_sb("onec_t", [128, 1], BF16)
        oner_t = g_sb("oner_t", [1, 128], F32)
        gs_t = g_sb("gs_t", [128, NGI], F32)
        us_t = g_sb("us_t", [128, NGI], F32)
        ds_t = g_sb("ds_t", [128, NCC], F32)

        s_init = sem("s_init")
        for dst, src in ((consts_t, consts), (onec_t, onec), (oner_t, oner),
                         (gs_t, gs), (us_t, us), (ds_t, ds)):
            p.op("sync", lambda e, d=dst, s=src: e.dma_start(out=d[:], in_=s),
                 inc=s_init, n=16)
        fence_init = [(s_init, s_init.v)]
        c_expbias = consts_t[:, 0:1]
        c_m128 = consts_t[:, 1:2]
        c_eps = consts_t[0:1, 2:3]

        s_wl = [sem("s_wl0"), sem("s_wl1")]
        s_wc = sem("s_wc")

        def build_w_slab(src, slab, stg, s_stgfree, wait_first=()):
            npc = 8
            cols = (NCC * D) // npc
            src3 = src.rearrange("(cc p) m -> p cc m", p=128)
            for i in range(npc):
                w0 = i * cols
                par = i % 2
                p.wait("sync", s_wc, max(0, s_wc.v - 1))
                if i == 0:
                    for s, v in wait_first:
                        p.wait("sync", s, v)
                    if s_stgfree is not None:
                        p.wait("sync", s_stgfree[0], s_stgfree[0].v)
                        p.wait("sync", s_stgfree[1], s_stgfree[1].v)
                p.op("sync", lambda e, a=stg[par], b=src3, i_=i, c=cols:
                     e.dma_start(
                         out=a[:, 0:c].rearrange("p (cc m) -> p cc m", cc=2),
                         in_=b[:, 2 * i_:2 * i_ + 2, :]),
                     inc=s_wl[par], n=16)
                p.wait("vector", s_wl[par], s_wl[par].v)
                if i == 0:
                    for s, v in wait_first:
                        p.wait("vector", s, v)
                p.op("vector", lambda e, o=w0, c=cols, a=stg[par], sl=slab:
                     e.tensor_copy(sl[:, o:o + c], a[:, 0:c]), inc=s_wc)
            return [(s_wc, s_wc.v)]

        with ExitStack() as pj:
            def pj_sb(name, shape, dt):
                return pj.enter_context(nc.sbuf_tensor(name, shape, dt))
            xf = [pj_sb("xf0", [128, 4096], F32), pj_sb("xf1", [128, 4096], F32)]
            s_xf = [sem("s_xf0"), sem("s_xf1")]
            s_xfc = [sem("s_xfc0"), sem("s_xfc1")]
            wslab = pj_sb("wslab", [128, NCC * D], BF16)

            def proj_phase(tag, xdram, ttot, ropetab, spill, prev_fence):
                ntq = ttot // 512
                if q_ntq is not None and tag == "q":
                    ntq = q_ntq
                with ExitStack() as ph:
                    def sb(name, shape, dt):
                        return ph.enter_context(
                            nc.sbuf_tensor(tag + name, shape, dt))
                    xb = [sb("xb0", [128, NCC * 512], BF16),
                          sb("xb1", [128, NCC * 512], BF16)]
                    stage = [sb("st0", [128, 8 * 512], BF16),
                             sb("st1", [128, 8 * 512], BF16)]
                    tsh = [sb("tsh0", [64, 512], F32), sb("tsh1", [64, 512], F32)]
                    tm1 = [sb("tm10", [64, 512], F32), sb("tm11", [64, 512], F32)]
                    tm2 = [sb("tm20", [64, 512], F32), sb("tm21", [64, 512], F32)]
                    sstrip = [sb("ss0", [64, 512], F32), sb("ss1", [64, 512], F32)]
                    rtab = sb("rtab", [128, ttot], F32)
                    psK = [ph.enter_context(
                               nc.psum_tensor(tag + "psK0", [128, 512], F32)),
                           ph.enter_context(
                               nc.psum_tensor(tag + "psK1", [128, 512], F32))]

                    s_tab = sem(tag + "s_tab")
                    s_mm = sem(tag + "s_mm")
                    s_shA = sem(tag + "s_shA")
                    s_t1 = sem(tag + "s_t1")
                    s_ropeD = sem(tag + "s_ropeD")
                    s_ropeA = sem(tag + "s_ropeA")
                    s_strip = sem(tag + "s_strip")
                    s_sp = [sem(tag + "s_sp0"), sem(tag + "s_sp1")]
                    s_xbf = sem(tag + "s_xbf")

                    for s, v in prev_fence:
                        p.wait("sync", s, v)
                    p.op("sync", lambda e: e.dma_start(out=rtab[:], in_=ropetab),
                         inc=s_tab, n=16)
                    TABV = s_tab.v

                    x3 = xdram.rearrange("(cc p) t -> p cc t", p=128)
                    first_dve_tab = True

                    for tq in range(ntq):
                        par = tq % 2
                        for hf in range(2):
                            xpar = (2 * tq + hf) % 2
                            p.wait("sync", s_xfc[xpar], s_xfc[xpar].v)
                            t0 = tq * 512 + hf * 256
                            p.op("sync", lambda e, a=xf[xpar], b=x3, o=t0:
                                 e.dma_start(
                                     out=a[:].rearrange("p (cc t) -> p cc t",
                                                        cc=NCC),
                                     in_=b[:, :, o:o + 256]),
                                 inc=s_xf[xpar], n=16)
                            p.wait("vector", s_xf[xpar], s_xf[xpar].v)
                            p.wait("vector", s_xbf, max(0, s_xbf.v - 1))
                            p.op("vector", lambda e, a=xb[par], b=xf[xpar], hh=hf:
                                 e.tensor_copy(
                                     a[:].rearrange("p (cc t) -> p cc t",
                                                    cc=NCC)[
                                         :, :, hh * 256:(hh + 1) * 256],
                                     b[:].rearrange("p (cc t) -> p cc t",
                                                    cc=NCC)),
                                 inc=s_xfc[xpar])

                        p.wait("gpsimd", s_t1, max(0, s_t1.v - 16))
                        if tq == 0:
                            p.wait("gpsimd", s_tab, TABV)
                        p.op("gpsimd", lambda e, a=sstrip[tq % 2], t=tq:
                             e.tensor_copy(a[:],
                                           rtab[64:128,
                                                t * 512:(t + 1) * 512]),
                             inc=s_strip)

                        for h in range(NH if not (q_nh and tag == "q")
                                       else q_nh):
                            hpar = h % 2
                            hb = h // 8
                            hloc = h % 8
                            p.wait("tensor", s_xfc[0], s_xfc[0].v)
                            p.wait("tensor", s_xfc[1], s_xfc[1].v)
                            p.wait("tensor", s_ropeD, max(0, s_ropeD.v - 3))
                            p.wait("tensor", s_shA, max(0, s_shA.v - 1))
                            p.wait("tensor", s_ropeA, max(0, s_ropeA.v - 1))
                            for cc in range(NCC):
                                last = cc == NCC - 1
                                p.op("tensor",
                                     lambda e, o=psK[hpar], x_=xb[par], c=cc,
                                     hh=h, st=(cc == 0), sp=last:
                                     e.matmul(o[:],
                                              wslab[:, c * D + hh * 128:
                                                    c * D + hh * 128 + 128],
                                              x_[:, c * 512:(c + 1) * 512],
                                              start=st, stop=sp),
                                     inc=s_mm if last else None)
                            mmv = s_mm.v
                            sgv = s_sp[hb].v

                            p.wait("scalar", s_mm, mmv)
                            p.wait("scalar", s_t1, max(0, s_t1.v - 1))
                            p.op("scalar", lambda e, a=tsh[hpar], o=psK[hpar]:
                                 e.copy(a[0:32, :], o[32:64, :]))
                            p.op("scalar", lambda e, a=tsh[hpar], o=psK[hpar]:
                                 e.copy(a[32:64, :], o[0:32, :]), inc=s_shA)
                            p.wait("scalar", s_sp[hb], sgv)
                            p.op("scalar", lambda e, a=stage[hb], o=psK[hpar],
                                 hl=hloc:
                                 e.copy(a[64:128, hl * 512:hl * 512 + 512],
                                        o[64:128, :]), inc=s_ropeA)
                            p.wait("gpsimd", s_shA, s_shA.v)
                            p.wait("gpsimd", s_ropeD, max(0, s_ropeD.v - 2))
                            p.wait("gpsimd", s_strip, s_strip.v)
                            p.op("gpsimd", lambda e, a=tm1[hpar], b=tsh[hpar],
                                 c=sstrip[tq % 2]:
                                 e.tensor_mul(a[:], b[:], c[:]),
                                 inc=s_t1)
                            p.wait("vector", s_mm, mmv)
                            p.wait("vector", s_ropeD, max(0, s_ropeD.v - 2))
                            p.wait("vector", s_shA, s_shA.v)
                            p.wait("vector", s_ropeA, s_ropeA.v)
                            if first_dve_tab:
                                p.wait("vector", s_tab, TABV)
                                first_dve_tab = False
                            p.op("vector", lambda e, a=tm2[hpar], o=psK[hpar],
                                 t=tq:
                                 e.tensor_mul(a[:], o[0:64, :],
                                              rtab[0:64,
                                                   t * 512:(t + 1) * 512]),
                                 inc=s_ropeD)
                            p.wait("vector", s_t1, s_t1.v)
                            p.wait("vector", s_ropeD, s_ropeD.v)
                            p.wait("vector", s_sp[hb], sgv)
                            p.op("vector", lambda e, a=stage[hb], b=tm1[hpar],
                                 c=tm2[hpar], hl=hloc:
                                 e.tensor_add(a[0:64, hl * 512:hl * 512 + 512],
                                              b[:], c[:]), inc=s_ropeD)
                            if h == 7 or h == 15:
                                hbase = h - 7
                                p.wait("scalar", s_ropeD, s_ropeD.v)
                                p.wait("scalar", s_ropeA, s_ropeA.v)
                                sp3 = spill.rearrange("p (h t) -> p h t", h=NH)
                                p.op("scalar",
                                     lambda e, a=stage[hb], b=sp3, t=tq,
                                     hh=hbase:
                                     e.dma_start(
                                         out=b[:, hh:hh + 8,
                                               t * 512:(t + 1) * 512],
                                         in_=a[:].rearrange(
                                             "p (h t) -> p h t", h=8)),
                                     inc=s_sp[hb], n=16)
                        p.op("tensor", lambda e: e.nop(), inc=s_xbf)
                    return [(s_sp[0], s_sp[0].v), (s_sp[1], s_sp[1].v),
                            (s_xfc[0], s_xfc[0].v), (s_xfc[1], s_xfc[1].v)]

            def vproj_phase(prev_fence):
                ntq = T // 512
                with ExitStack() as ph:
                    def sb(name, shape, dt):
                        return ph.enter_context(
                            nc.sbuf_tensor("v" + name, shape, dt))
                    xb = [sb("xb0", [128, NCC * 512], BF16),
                          sb("xb1", [128, NCC * 512], BF16)]
                    vstage = [sb("vs0", [128, D], BF16),
                              sb("vs1", [128, D], BF16)]
                    psV = [ph.enter_context(
                               nc.psum_tensor("vpsV0", [128, D], F32)),
                           ph.enter_context(
                               nc.psum_tensor("vpsV1", [128, D], F32))]
                    s_mm = sem("vs_mm")
                    s_ev = sem("vs_ev")
                    s_sp = [sem("vs_sp0"), sem("vs_sp1")]
                    s_xbf = sem("vs_xbf")

                    x3 = xt.rearrange("(cc p) t -> p cc t", p=128)
                    for tq in range(ntq):
                        par = tq % 2
                        for hf in range(2):
                            xpar = (2 * tq + hf) % 2
                            p.wait("sync", s_xfc[xpar], s_xfc[xpar].v)
                            if tq == 0 and hf == 0:
                                for s, v in prev_fence:
                                    p.wait("sync", s, v)
                            t0 = tq * 512 + hf * 256
                            p.op("sync", lambda e, a=xf[xpar], b=x3, o=t0:
                                 e.dma_start(
                                     out=a[:].rearrange("p (cc t) -> p cc t",
                                                        cc=NCC),
                                     in_=b[:, :, o:o + 256]),
                                 inc=s_xf[xpar], n=16)
                            p.wait("vector", s_xf[xpar], s_xf[xpar].v)
                            p.wait("vector", s_xbf, max(0, s_xbf.v - 1))
                            if tq == 0 and hf == 0:
                                for s, v in prev_fence:
                                    p.wait("vector", s, v)
                            p.op("vector", lambda e, a=xb[par], b=xf[xpar],
                                 hh=hf:
                                 e.tensor_copy(
                                     a[:].rearrange("p (cc t) -> p cc t",
                                                    cc=NCC)[
                                         :, :, hh * 256:(hh + 1) * 256],
                                     b[:].rearrange("p (cc t) -> p cc t",
                                                    cc=NCC)),
                                 inc=s_xfc[xpar])

                        for ks in range(4):
                            vpar = ks % 2
                            kc = tq * 4 + ks
                            p.wait("tensor", s_xfc[0], s_xfc[0].v)
                            p.wait("tensor", s_xfc[1], s_xfc[1].v)
                            p.wait("tensor", s_ev, max(0, s_ev.v - 1))
                            for cc in range(NCC):
                                for nn in range(4):
                                    last = cc == NCC - 1 and nn == 3
                                    p.op("tensor",
                                         lambda e, o=psV[vpar], x_=xb[par],
                                         c=cc, k=ks, n_=nn, st=(cc == 0),
                                         sp=(cc == NCC - 1):
                                         e.matmul(
                                             o[:, n_ * 512:(n_ + 1) * 512],
                                             x_[:, c * 512 + k * 128:
                                                c * 512 + k * 128 + 128],
                                             wslab[:, c * D + n_ * 512:
                                                   c * D + (n_ + 1) * 512],
                                             start=st, stop=sp),
                                         inc=s_mm if last else None)
                            p.wait("vector", s_mm, s_mm.v)
                            p.wait("vector", s_sp[vpar], s_sp[vpar].v)
                            p.op("vector", lambda e, a=vstage[vpar],
                                 o=psV[vpar]:
                                 e.tensor_copy(a[:], o[:]), inc=s_ev)
                            p.wait("scalar", s_ev, s_ev.v)
                            p.op("scalar", lambda e, a=vstage[vpar], k=kc:
                                 e.dma_start(
                                     out=vspill[:, :, k, :],
                                     in_=a[:].rearrange("p (h d) -> p h d",
                                                        h=NH)),
                                 inc=s_sp[vpar], n=16)
                        p.op("tensor", lambda e: e.nop(), inc=s_xbf)
                    return [(s_sp[0], s_sp[0].v), (s_sp[1], s_sp[1].v),
                            (s_xfc[0], s_xfc[0].v), (s_xfc[1], s_xfc[1].v)]

            fw = build_w_slab(wq, wslab, xf, None, wait_first=fence_init)
            if upto == "w":
                for s, v in fw:
                    p.wait("sync", s, v)
                with nc.Block() as block:
                    p.flush(block)
                return nc
            fq = proj_phase("q", xq, TLOC, ropeq, qspill, fw)
            if upto == "q":
                for s, v in fq:
                    p.wait("sync", s, v)
            else:
                fw = build_w_slab(wk, wslab, xf, s_xfc, wait_first=fq)
                fk = proj_phase("k", xt, T, ropek, kspill, fw)
                if upto == "k":
                    for s, v in fk:
                        p.wait("sync", s, v)
                else:
                    fw = build_w_slab(wv, wslab, xf, s_xfc, wait_first=fk)
                    fv = vproj_phase(fw)
            fproj = (fq[:2] + fk[:2] + fv) if upto not in ("q", "k") else []

        if upto in ("q", "k", "proj"):
            if upto == "proj":
                for s, v in fproj:
                    p.wait("sync", s, v)
            with nc.Block() as block:
                p.flush(block)
            if verbose:
                print("instruction queue sizes:", p.counts())
            return nc

        with ExitStack() as ao:
            Yall = ao.enter_context(
                nc.sbuf_tensor("Yall", [128, NH * TLOC], BF16))
            s_yall = sem("s_yall")
            with ExitStack() as ph:
                def sb(name, shape, dt):
                    return ph.enter_context(nc.sbuf_tensor("a" + name, shape, dt))
                kb = [sb("kb0", [128, T], BF16), sb("kb1", [128, T], BF16)]
                vb = [sb("vb0", [128, T], BF16), sb("vb1", [128, T], BF16)]
                qb = [sb("qb0", [128, TLOC], BF16), sb("qb1", [128, TLOC], BF16)]
                at = [sb(f"at{i}", [128, 1024], BF16) for i in range(4)]
                rr = sb("rr", [1, 1024], F32)
                rb = sb("rb", [128, 1024], F32)
                acc = sb("acc", [128, 1024], F32)
                accb = sb("accb", [128, 1024], BF16)
                psS = [ph.enter_context(nc.psum_tensor("apsS0", [128, 1024], F32)),
                       ph.enter_context(nc.psum_tensor("apsS1", [128, 1024], F32))]
                psY = ph.enter_context(nc.psum_tensor("apsY", [128, 1024], F32))
                psAux = ph.enter_context(nc.psum_tensor("apsAux", [128, 1024], F32))

                s_ld = [sem("a_ld0"), sem("a_ld1")]
                s_S = sem("a_S")
                s_exp = sem("a_exp")
                s_AV = sem("a_AV")
                s_acc = sem("a_acc")
                s_accb = sem("a_accb")
                s_sum = sem("a_sum")
                s_rec = sem("a_rec")
                s_bc = sem("a_bc")
                s_rbc = sem("a_rbc")
                s_hd = sem("a_hd")

                for h in range(NH):
                    par = h % 2
                    if h == 0:
                        for s, v in fproj:
                            p.wait("sync", s, v)
                    p.wait("sync", s_hd, max(0, s_hd.v - 1))
                    p.op("sync", lambda e, a=kb[par], hh=h:
                         e.dma_start(out=a[:],
                                     in_=kspill[:, hh * T:(hh + 1) * T]),
                         inc=s_ld[par], n=16)
                    p.op("sync", lambda e, a=vb[par], hh=h:
                         e.dma_start(
                             out=a[:].rearrange("p (kc d) -> p kc d", kc=NKC),
                             in_=vspill[:, hh, :, :]),
                         inc=s_ld[par], n=16)
                    p.op("sync", lambda e, a=qb[par], hh=h:
                         e.dma_start(out=a[:],
                                     in_=qspill[:, hh * TLOC:(hh + 1) * TLOC]),
                         inc=s_ld[par], n=16)
                    ldv = s_ld[par].v

                    for sw in range(2):
                        q0 = sw * 1024

                        def emit_S(kc):
                            spar = kc % 2
                            for half in range(2):
                                p.op("tensor",
                                     lambda e, o=psS[spar], k_=kb[par],
                                     q_=qb[par], c=kc, hh=half, qo=q0:
                                     e.matmul(o[:, hh * 512:(hh + 1) * 512],
                                              k_[:, c * 128:(c + 1) * 128],
                                              q_[:, qo + hh * 512:
                                                 qo + (hh + 1) * 512],
                                              start=True, stop=True),
                                     inc=s_S if half == 1 else None)
                            return s_S.v

                        p.wait("tensor", s_ld[par], ldv)
                        p.wait("tensor", s_exp, s_exp.v)
                        sval = [emit_S(0)]
                        p.wait("tensor", s_yall, s_yall.v)
                        p.wait("tensor", s_rbc, s_rbc.v)
                        accv = {}
                        for kc in range(NKC):
                            spar = kc % 2
                            if kc + 1 < NKC:
                                p.wait("tensor", s_exp, s_exp.v)
                                sval.append(emit_S(kc + 1))
                            if kc >= 1 and (kc - 1) % 2 == 0:
                                for half in range(2):
                                    p.op("tensor",
                                         lambda e, a=at[(kc - 1) % 4],
                                         hh=half, st=(kc == 1):
                                         e.matmul(
                                             psAux[0:1,
                                                   hh * 512:(hh + 1) * 512],
                                             onec_t[:],
                                             a[:, hh * 512:(hh + 1) * 512],
                                             start=st, stop=False),
                                         inc=s_AV if half == 1 else None)
                            apar = kc % 4
                            p.wait("scalar", s_S, sval[kc])
                            if kc >= 4 and (kc - 4) % 2 == 1:
                                p.wait("scalar", s_acc, accv[kc - 4])
                            p.op("scalar", lambda e, a=at[apar], o=psS[spar]:
                                 e.activation(a[:], o[:], AF.Exp,
                                              bias=c_expbias, scale=SCL),
                                 inc=s_exp)
                            p.wait("tensor", s_exp, s_exp.v)
                            for half in range(2):
                                p.op("tensor",
                                     lambda e, v_=vb[par], a=at[apar], c=kc,
                                     hh=half:
                                     e.matmul(psY[:, hh * 512:(hh + 1) * 512],
                                              v_[:, c * 128:(c + 1) * 128],
                                              a[:, hh * 512:(hh + 1) * 512],
                                              start=(c == 0),
                                              stop=(c == NKC - 1)),
                                     inc=s_AV if half == 1 else None)
                            if kc % 2 == 1:
                                p.wait("vector", s_exp, s_exp.v)
                                p.wait("vector", s_acc, s_acc.v)
                                if kc == 1:
                                    p.wait("vector", s_accb, s_accb.v)
                                    p.op("vector", lambda e, a=at[apar]:
                                         e.tensor_copy(acc[:], a[:]),
                                         inc=s_acc)
                                else:
                                    p.op("vector", lambda e, a=at[apar]:
                                         e.tensor_add(acc[:], acc[:], a[:]),
                                         inc=s_acc)
                            accv[kc] = s_acc.v
                        p.wait("vector", s_acc, s_acc.v)
                        p.wait("vector", s_sum, s_sum.v)
                        p.op("vector", lambda e: e.tensor_copy(accb[:], acc[:]),
                             inc=s_accb)
                        p.wait("tensor", s_accb, s_accb.v)
                        for half in range(2):
                            p.op("tensor", lambda e, hh=half:
                                 e.matmul(psAux[0:1, hh * 512:(hh + 1) * 512],
                                          onec_t[:],
                                          accb[:, hh * 512:(hh + 1) * 512],
                                          start=False, stop=True),
                                 inc=s_sum if half == 1 else None)
                        p.wait("vector", s_sum, s_sum.v)
                        p.wait("vector", s_bc, s_bc.v)
                        p.op("vector", lambda e: e.reciprocal(rr[:],
                                                              psAux[0:1, :]),
                             inc=s_rec)
                        p.wait("tensor", s_rec, s_rec.v)
                        for half in range(2):
                            p.op("tensor", lambda e, hh=half:
                                 e.matmul(psAux[:, hh * 512:(hh + 1) * 512],
                                          oner_t[:],
                                          rr[0:1, hh * 512:(hh + 1) * 512],
                                          start=True, stop=True),
                                 inc=s_bc if half == 1 else None)
                        p.wait("vector", s_bc, s_bc.v)
                        p.wait("vector", s_yall, s_yall.v)
                        p.op("vector", lambda e: e.tensor_copy(rb[:], psAux[:]),
                             inc=s_rbc)
                        p.wait("vector", s_AV, s_AV.v)
                        p.wait("vector", s_rbc, s_rbc.v)
                        p.op("vector", lambda e, hh=h, qo=q0:
                             e.tensor_mul(
                                 Yall[:, hh * TLOC + qo:hh * TLOC + qo + 1024],
                                 psY[:], rb[:]),
                             inc=s_yall)
                    p.op("tensor", lambda e: e.nop(), inc=s_hd)
                fatt = [(s_hd, s_hd.v), (s_yall, s_yall.v)]
                if upto == "att":
                    p.wait("sync", s_hd, s_hd.v)
                    with nc.Block() as block:
                        p.flush(block)
                    if verbose:
                        print("instruction queue sizes:", p.counts())
                    return nc

            def ln_rows(psMu, psM2, rows, rstdr, psBc, abslab, S, abc_guards):
                s_stats, s_rows, s_rstd, s_bcmm, s_abc = S
                p.wait("vector", s_stats, s_stats.v)
                p.op("vector", lambda e: e.tensor_scalar_mul(
                    rows[0:1, 0:512], psMu[0:1, 0:512], 1.0 / D), inc=s_rows)
                p.op("vector", lambda e: e.tensor_scalar_mul(
                    rows[0:1, 512:1024], psM2[0:1, 0:512], 1.0 / D), inc=s_rows)
                p.wait("vector", s_rows, s_rows.v)
                p.op("vector", lambda e: e.tensor_mul(
                    rows[0:1, 1024:1536], rows[0:1, 0:512], rows[0:1, 0:512]),
                    inc=s_rows)
                p.wait("vector", s_rows, s_rows.v)
                p.op("vector", lambda e: e.tensor_sub(
                    rows[0:1, 1536:2048], rows[0:1, 512:1024],
                    rows[0:1, 1024:1536]), inc=s_rows)
                p.wait("scalar", s_rows, s_rows.v)
                p.op("scalar", lambda e: e.activation(
                    rstdr[0:1, 0:512], rows[0:1, 1536:2048], AF.Ln,
                    bias=c_eps), inc=s_rstd)
                p.wait("scalar", s_rstd, s_rstd.v)
                p.op("scalar", lambda e: e.activation(
                    rstdr[0:1, 512:1024], rstdr[0:1, 0:512], AF.Exp,
                    scale=-0.5), inc=s_rstd)
                p.wait("vector", s_rstd, s_rstd.v)
                p.op("vector", lambda e: e.scalar_tensor_tensor(
                    rstdr[0:1, 1024:1536], rows[0:1, 0:512], -1.0,
                    rstdr[0:1, 512:1024], ALU.mult, ALU.mult), inc=s_rows)
                p.wait("tensor", s_rows, s_rows.v)
                p.wait("tensor", s_rstd, s_rstd.v)
                p.wait("tensor", s_abc, s_abc.v)
                p.op("tensor", lambda e: e.matmul(
                    psBc[:, 0:512], oner_t[:], rstdr[0:1, 512:1024],
                    start=True, stop=True), inc=None)
                p.op("tensor", lambda e: e.matmul(
                    psBc[:, 512:1024], oner_t[:], rstdr[0:1, 1024:1536],
                    start=True, stop=True), inc=s_bcmm)
                p.wait("vector", s_bcmm, s_bcmm.v)
                for s, v in abc_guards:
                    p.wait("vector", s, v)
                p.op("vector", lambda e: e.tensor_copy(abslab[:], psBc[:]),
                     inc=s_abc)

            with ExitStack() as ph:
                def sb(name, shape, dt):
                    return ph.enter_context(nc.sbuf_tensor("o" + name, shape, dt))
                wslab2 = sb("wslab2", [128, NCC * D], BF16)
                with ExitStack() as stgsc:
                    stg2 = [stgsc.enter_context(
                                nc.sbuf_tensor("ostg0", [128, 4096], F32)),
                            stgsc.enter_context(
                                nc.sbuf_tensor("ostg1", [128, 4096], F32))]
                    fw = build_w_slab(wo, wslab2, stg2, None,
                                      wait_first=fatt)
                WOC = fw[0][1]
                sslab = sb("sslab", [128, NCC * 512], F32)
                x2b16 = sb("x2b16", [128, NCC * 512], BF16)
                sb16 = [sb("sb16_0", [128, 512], BF16),
                        sb("sb16_1", [128, 512], BF16)]
                sqb = [sb("sqb0", [128, 512], BF16), sb("sqb1", [128, 512], BF16)]
                xres = [sb("xres0", [128, 512], F32), sb("xres1", [128, 512], F32)]
                rows = sb("rows", [1, 2048], F32)
                rstdr = sb("rstdr", [1, 1536], F32)
                abslab = sb("abslab", [128, 1024], F32)
                psO = [ph.enter_context(nc.psum_tensor("opsO0", [128, 512], F32)),
                       ph.enter_context(nc.psum_tensor("opsO1", [128, 512], F32))]
                psMu = ph.enter_context(nc.psum_tensor("opsMu", [128, 512], F32))
                psM2 = ph.enter_context(nc.psum_tensor("opsM2", [128, 512], F32))
                psBc = ph.enter_context(nc.psum_tensor("opsBc", [128, 1024], F32))

                s_xr = [sem("o_xr0"), sem("o_xr1")]
                s_mm = sem("o_mm")
                s_ev = sem("o_ev")
                s_c16 = sem("o_c16")
                s_sq = sem("o_sq")
                s_stats = sem("o_stats")
                s_rows = sem("o_rows")
                s_rstd = sem("o_rstd")
                s_bcmm = sem("o_bcmm")
                s_abc = sem("o_abc")
                s_x2 = sem("o_x2")
                s_x2c = sem("o_x2c")
                s_spf = sem("o_spf")
                s_spb = sem("o_spb")

                xq3 = xq.rearrange("(cc p) t -> p cc t", p=128)
                for tc in range(4):
                    for cc in range(NCC):
                        par = cc % 2
                        p.wait("sync", s_ev, max(0, s_ev.v - 1))
                        if tc == 0 and cc == 0:
                            for s, v in fw:
                                p.wait("sync", s, v)
                        p.op("sync", lambda e, a=xres[par], c=cc, t=tc:
                             e.dma_start(out=a[:],
                                         in_=xq3[:, c, t * 512:(t + 1) * 512]),
                             inc=s_xr[par], n=16)
                        if tc == 0 and cc == 0:
                            p.wait("tensor", s_wc, WOC)
                            p.wait("tensor", s_yall, s_yall.v)
                        p.wait("tensor", s_ev, max(0, s_ev.v - 1))
                        for h in range(NH):
                            p.op("tensor",
                                 lambda e, o=psO[par], hh=h, c=cc, t=tc,
                                 st=(h == 0), sp=(h == NH - 1):
                                 e.matmul(o[:],
                                          wslab2[:, hh * D + c * 128:
                                                 hh * D + c * 128 + 128],
                                          Yall[:, hh * TLOC + t * 512:
                                               hh * TLOC + (t + 1) * 512],
                                          start=st, stop=sp),
                                 inc=s_mm if h == NH - 1 else None)
                        p.wait("vector", s_mm, s_mm.v)
                        p.wait("vector", s_xr[par], s_xr[par].v)
                        if tc > 0 and cc == 0:
                            p.wait("vector", s_spf, s_spf.v)
                            p.wait("vector", s_spb, s_spb.v)
                        p.op("vector", lambda e, c=cc, par_=par:
                             e.scalar_tensor_tensor(
                                 sslab[:, c * 512:(c + 1) * 512], psO[par_][:],
                                 1.0, xres[par_][:], ALU.bypass, ALU.add),
                             inc=s_ev)
                        p.wait("scalar", s_ev, s_ev.v)
                        p.wait("scalar", s_stats, max(0, s_stats.v - 1))
                        p.op("scalar", lambda e, c=cc, par_=par:
                             e.copy(sb16[par_][:],
                                    sslab[:, c * 512:(c + 1) * 512]),
                             inc=s_c16)
                        p.wait("vector", s_c16, s_c16.v)
                        p.wait("vector", s_stats, max(0, s_stats.v - 1))
                        p.op("vector", lambda e, par_=par:
                             e.tensor_mul(sqb[par_][:], sb16[par_][:],
                                          sb16[par_][:]), inc=s_sq)
                        p.wait("tensor", s_c16, s_c16.v)
                        p.op("tensor", lambda e, c=cc, par_=par:
                             e.matmul(psMu[0:1, :], onec_t[:], sb16[par_][:],
                                      start=(c == 0), stop=(c == NCC - 1)),
                             inc=None)
                        p.wait("tensor", s_sq, s_sq.v)
                        p.op("tensor", lambda e, c=cc, par_=par:
                             e.matmul(psM2[0:1, :], onec_t[:], sqb[par_][:],
                                      start=(c == 0), stop=(c == NCC - 1)),
                             inc=s_stats)
                    ln_rows(psMu, psM2, rows, rstdr, psBc, abslab,
                            (s_stats, s_rows, s_rstd, s_bcmm, s_abc),
                            [(s_x2, s_x2.v)])
                    p.wait("vector", s_abc, s_abc.v)
                    for cc in range(NCC):
                        p.op("vector", lambda e, c=cc:
                             e.tensor_mul(sslab[:, c * 512:(c + 1) * 512],
                                          sslab[:, c * 512:(c + 1) * 512],
                                          abslab[:, 0:512]), inc=s_x2)
                        p.wait("vector", s_x2, s_x2.v)
                        p.op("vector", lambda e, c=cc:
                             e.tensor_add(sslab[:, c * 512:(c + 1) * 512],
                                          sslab[:, c * 512:(c + 1) * 512],
                                          abslab[:, 512:1024]), inc=s_x2)
                        p.wait("scalar", s_x2, s_x2.v)
                        p.op("scalar", lambda e, c=cc:
                             e.copy(x2b16[:, c * 512:(c + 1) * 512],
                                    sslab[:, c * 512:(c + 1) * 512]),
                             inc=s_x2c)
                    p.wait("sync", s_x2, s_x2.v)
                    p.op("sync", lambda e, t=tc:
                         e.dma_start(
                             out=x2spill[:, :, t * 512:(t + 1) * 512],
                             in_=sslab[:].rearrange("p (cc t) -> p cc t",
                                                    cc=NCC)),
                         inc=s_spf, n=16)
                    p.wait("sync", s_x2c, s_x2c.v)
                    p.op("sync", lambda e, t=tc:
                         e.dma_start(
                             out=x2bspill[:, :, t * 512:(t + 1) * 512],
                             in_=x2b16[:].rearrange("p (cc t) -> p cc t",
                                                    cc=NCC)),
                         inc=s_spb, n=16)
                fout = [(s_spf, s_spf.v), (s_spb, s_spb.v)]
                if upto == "out":
                    for s, v in fout:
                        p.wait("sync", s, v)
                    with nc.Block() as block:
                        p.flush(block)
                    if verbose:
                        print("instruction queue sizes:", p.counts())
                    return nc

        s_yst = sem("s_yst")
        NIW = 256
        NIU = II // NIW
        with ExitStack() as ph:
            def sb(name, shape, dt):
                return ph.enter_context(nc.sbuf_tensor("f" + name, shape, dt))
            hsl = sb("hsl", [128, NGI * 512], BF16)
            x2b = sb("x2b", [128, NCC * 512], BF16)
            wg8 = [sb("wg80", [128, NCC * NIW], U8),
                   sb("wg81", [128, NCC * NIW], U8)]
            wu8 = [sb("wu80", [128, NCC * NIW], U8),
                   sb("wu81", [128, NCC * NIW], U8)]
            wgb = [sb("wgb0", [128, NCC * NIW], BF16),
                   sb("wgb1", [128, NCC * NIW], BF16)]
            wub = [sb("wub0", [128, NCC * NIW], BF16),
                   sb("wub1", [128, NCC * NIW], BF16)]
            sgb = [sb("sgb0", [128, 512], BF16), sb("sgb1", [128, 512], BF16)]
            sub = [sb("sub0", [128, 512], BF16), sb("sub1", [128, 512], BF16)]
            wd8 = [sb(f"wd8{i}", [128, 4 * 512], U8) for i in range(2)]
            wdb = [sb(f"wdb{i}", [128, 4 * 512], BF16) for i in range(2)]
            f_sslab = sb("f_sslab", [128, NCC * 512], F32)
            f_sb16 = [sb("sb16_0", [128, 512], BF16), sb("sb16_1", [128, 512], BF16)]
            f_sqb = [sb("sqb0", [128, 512], BF16), sb("sqb1", [128, 512], BF16)]
            f_xres = [sb("xres0", [128, 512], F32), sb("xres1", [128, 512], F32)]
            f_rows = sb("f_rows", [1, 2048], F32)
            f_rstdr = sb("f_rstdr", [1, 1536], F32)
            f_abslab = sb("f_abslab", [128, 1024], F32)

            s_x2l = sem("f_x2l")
            s_g8 = [sem("f_g80"), sem("f_g81")]
            s_u8 = [sem("f_u80"), sem("f_u81")]
            s_gc = [sem("f_gc0"), sem("f_gc1")]
            s_uc = [sem("f_uc0"), sem("f_uc1")]
            s_gu = sem("f_gu")
            s_sil = sem("f_sil")
            s_usc = sem("f_usc")
            s_h = sem("f_h")
            s_d8 = [sem(f"f_d8{i}") for i in range(2)]
            s_dc = [sem(f"f_dc{i}") for i in range(2)]
            s_dgi = sem("f_dgi")
            f_s_ev = sem("f_ev")
            f_s_xr = [sem("f_xr0"), sem("f_xr1")]
            f_s_c16 = sem("f_c16")
            f_s_sq = sem("f_sq")
            f_s_stats = sem("f_stats")
            f_s_rows = sem("f_rows")
            f_s_rstd = sem("f_rstd")
            f_s_bcmm = sem("f_bcmm")
            f_s_abc = sem("f_abc")
            s_x2ap = sem("f_x2ap")

            wg3 = wg.rearrange("(cc p) i -> p cc i", p=128)
            wu3 = wu.rearrange("(cc p) i -> p cc i", p=128)
            NIL = NIW // 128

            for tc in range(4):
                with ExitStack() as gsc:
                    psG = [gsc.enter_context(
                               nc.psum_tensor(f"fpsG{tc}_0", [128, 512], F32)),
                           gsc.enter_context(
                               nc.psum_tensor(f"fpsG{tc}_1", [128, 512], F32))]
                    psU = [gsc.enter_context(
                               nc.psum_tensor(f"fpsU{tc}_0", [128, 512], F32)),
                           gsc.enter_context(
                               nc.psum_tensor(f"fpsU{tc}_1", [128, 512], F32))]

                    p.wait("sync", s_gu, s_gu.v)
                    if tc == 0:
                        for s, v in fout:
                            p.wait("sync", s, v)
                    p.op("sync", lambda e, t=tc:
                         e.dma_start(
                             out=x2b[:].rearrange("p (cc t) -> p cc t", cc=NCC),
                             in_=x2bspill[:, :, t * 512:(t + 1) * 512]),
                         inc=s_x2l, n=16)
                    def emit_gu_prefetch(iu):
                        par = iu % 2
                        i0 = iu * NIW
                        guard = max(0, s_gu.v - (NIL if iu == 0 else 0))
                        p.wait("sync", s_gc[par], s_gc[par].v)
                        p.op("sync", lambda e, a=wg8[par], o=i0:
                             e.dma_start(
                                 out=a[:].rearrange("p (cc i) -> p cc i",
                                                    cc=NCC),
                                 in_=wg3[:, :, o:o + NIW]),
                             inc=s_g8[par], n=16)
                        p.wait("sync", s_uc[par], s_uc[par].v)
                        p.op("sync", lambda e, a=wu8[par], o=i0:
                             e.dma_start(
                                 out=a[:].rearrange("p (cc i) -> p cc i",
                                                    cc=NCC),
                                 in_=wu3[:, :, o:o + NIW]),
                             inc=s_u8[par], n=16)
                        p.wait("vector", s_g8[par], s_g8[par].v)
                        p.wait("vector", s_gu, guard)
                        p.op("vector", lambda e, a=wgb[par], b=wg8[par]:
                             e.tensor_scalar(a[:], b[:], -128.0, None,
                                             ALU.add), inc=s_gc[par])
                        p.wait("scalar", s_u8[par], s_u8[par].v)
                        p.wait("scalar", s_gu, guard)
                        p.op("scalar", lambda e, a=wub[par], b=wu8[par]:
                             e.activation(a[:], b[:], AF.Copy, bias=-128.0),
                             inc=s_uc[par])

                    emit_gu_prefetch(0)
                    for iu in range(NIU):
                        par = iu % 2
                        if iu + 1 < NIU:
                            emit_gu_prefetch(iu + 1)
                        for il in range(NIL):
                            gi = iu * NIL + il
                            gpar = gi % 2
                            p.wait("tensor", s_gc[par], s_gc[par].v)
                            p.wait("tensor", s_uc[par], s_uc[par].v)
                            if gi == 0:
                                p.wait("tensor", s_x2l, s_x2l.v)
                                if tc > 0:
                                    p.wait("tensor", f_s_ev, f_s_ev.v)
                                    p.wait("tensor", f_s_abc, f_s_abc.v)
                            p.wait("tensor", s_sil, max(0, s_sil.v - 1))
                            p.wait("tensor", s_usc, max(0, s_usc.v - 1))
                            for cc in range(NCC):
                                p.op("tensor",
                                     lambda e, o=psG[gpar], w=wgb[par], c=cc,
                                     i=il, st=(cc == 0), sp=(cc == NCC - 1):
                                     e.matmul(o[:],
                                              w[:, c * NIW + i * 128:
                                                c * NIW + i * 128 + 128],
                                              x2b[:, c * 512:(c + 1) * 512],
                                              start=st, stop=sp),
                                     inc=None)
                            for cc in range(NCC):
                                p.op("tensor",
                                     lambda e, o=psU[gpar], w=wub[par], c=cc,
                                     i=il, st=(cc == 0), sp=(cc == NCC - 1):
                                     e.matmul(o[:],
                                              w[:, c * NIW + i * 128:
                                                c * NIW + i * 128 + 128],
                                              x2b[:, c * 512:(c + 1) * 512],
                                              start=st, stop=sp),
                                     inc=s_gu if cc == NCC - 1 else None)
                            p.wait("scalar", s_gu, s_gu.v)
                            p.wait("scalar", s_h, max(0, s_h.v - 1))
                            p.op("scalar", lambda e, o=psG[gpar], g=gi,
                                 par_=gpar:
                                 e.activation(sgb[par_][:], o[:], AF.Silu,
                                              scale=gs_t[:, g:g + 1]),
                                 inc=s_sil)
                            p.op("scalar", lambda e, o=psU[gpar], g=gi,
                                 par_=gpar:
                                 e.activation(sub[par_][:], o[:], AF.Copy,
                                              scale=us_t[:, g:g + 1]),
                                 inc=s_usc)
                            p.wait("vector", s_sil, s_sil.v)
                            p.wait("vector", s_usc, s_usc.v)
                            if tc > 0 and gi == 0:
                                p.wait("vector", s_dgi, s_dgi.v)
                            p.op("vector", lambda e, g=gi, par_=gpar:
                                 e.tensor_mul(hsl[:, g * 512:(g + 1) * 512],
                                              sgb[par_][:], sub[par_][:]),
                                 inc=s_h)
                    HC = s_h.v

                with ExitStack() as dsc:
                    psD = dsc.enter_context(
                        nc.psum_tensor(f"fpsD{tc}", [128, 2048], F32))
                    f_psMu = dsc.enter_context(
                        nc.psum_tensor(f"fpsMu{tc}", [128, 512], F32))
                    f_psM2 = dsc.enter_context(
                        nc.psum_tensor(f"fpsM2{tc}", [128, 512], F32))
                    f_psBc = dsc.enter_context(
                        nc.psum_tensor(f"fpsBc{tc}", [128, 1024], F32))

                    for cg in range(4):
                        for gb in range(NGI // 4):
                            bpar = gb % 2
                            ring = "sync" if gb % 2 == 0 else "scalar"
                            p.wait(ring, s_dc[bpar], s_dc[bpar].v)
                            p.op(ring, lambda e, a=wd8[bpar], g=gb, c=cg:
                                 e.dma_start(
                                     out=a[:].rearrange(
                                         "p (gs c) -> p gs c", gs=4),
                                     in_=wd[g * 512:(g + 1) * 512,
                                            c * 512:(c + 1) * 512].rearrange(
                                         "(gs p) c -> p gs c", p=128)),
                                 inc=s_d8[bpar], n=16)
                            p.wait("scalar", s_d8[bpar], s_d8[bpar].v)
                            p.wait("scalar", s_dgi, max(0, s_dgi.v - 4))
                            p.op("scalar", lambda e, a=wdb[bpar], b=wd8[bpar]:
                                 e.activation(a[:], b[:], AF.Copy,
                                              bias=-128.0), inc=s_dc[bpar])
                            p.wait("tensor", s_dc[bpar], s_dc[bpar].v)
                            if gb == 0:
                                p.wait("tensor", s_h, HC)
                                p.wait("tensor", f_s_ev, f_s_ev.v)
                            for gs in range(4):
                                gi = gb * 4 + gs
                                glast = gi == NGI - 1
                                for cl in range(4):
                                    p.op("tensor",
                                         lambda e, ps=psD, w=wdb[bpar], c=cl,
                                         g=gi, g2=gs, st=(gi == 0), sp=glast:
                                         e.matmul(
                                             ps[:, c * 512:(c + 1) * 512],
                                             w[:, g2 * 512 + c * 128:
                                               g2 * 512 + c * 128 + 128],
                                             hsl[:, g * 512:(g + 1) * 512],
                                             start=st, stop=sp),
                                         inc=s_dgi if cl == 3 else None)
                        for cl in range(4):
                            cc = cg * 4 + cl
                            par = cc % 2
                            p.wait("sync", f_s_ev, max(0, f_s_ev.v - 1))
                            p.op("sync", lambda e, a=f_xres[par], c=cc, t=tc:
                                 e.dma_start(
                                     out=a[:],
                                     in_=x2spill[:, c,
                                                 t * 512:(t + 1) * 512]),
                                 inc=f_s_xr[par], n=16)
                            p.wait("vector", s_dgi, s_dgi.v)
                            p.wait("vector", f_s_xr[par], f_s_xr[par].v)
                            if tc > 0 and cc == 0:
                                p.wait("vector", s_yst, s_yst.v)
                            p.op("vector",
                                 lambda e, ps=psD, c=cc, cl_=cl, par_=par:
                                 e.scalar_tensor_tensor(
                                     f_sslab[:, c * 512:(c + 1) * 512],
                                     ps[:, cl_ * 512:(cl_ + 1) * 512],
                                     ds_t[:, c:c + 1], f_xres[par_][:],
                                     ALU.mult, ALU.add), inc=f_s_ev)
                            p.wait("scalar", f_s_ev, f_s_ev.v)
                            p.wait("scalar", f_s_stats, max(0, f_s_stats.v - 1))
                            p.op("scalar", lambda e, c=cc, par_=par:
                                 e.copy(f_sb16[par_][:],
                                        f_sslab[:, c * 512:(c + 1) * 512]),
                                 inc=f_s_c16)
                            p.wait("vector", f_s_c16, f_s_c16.v)
                            p.wait("vector", f_s_stats, max(0, f_s_stats.v - 1))
                            p.op("vector", lambda e, par_=par:
                                 e.tensor_mul(f_sqb[par_][:], f_sb16[par_][:],
                                              f_sb16[par_][:]), inc=f_s_sq)
                            p.wait("tensor", f_s_c16, f_s_c16.v)
                            p.op("tensor", lambda e, ps=f_psMu, c=cc, par_=par:
                                 e.matmul(ps[0:1, :], onec_t[:],
                                          f_sb16[par_][:], start=(c == 0),
                                          stop=(c == NCC - 1)), inc=None)
                            p.wait("tensor", f_s_sq, f_s_sq.v)
                            p.op("tensor", lambda e, ps=f_psM2, c=cc, par_=par:
                                 e.matmul(ps[0:1, :], onec_t[:],
                                          f_sqb[par_][:], start=(c == 0),
                                          stop=(c == NCC - 1)), inc=f_s_stats)
                    ln_rows(f_psMu, f_psM2, f_rows, f_rstdr, f_psBc, f_abslab,
                            (f_s_stats, f_s_rows, f_s_rstd, f_s_bcmm, f_s_abc),
                            [(s_x2ap, s_x2ap.v)])
                    p.wait("vector", f_s_abc, f_s_abc.v)
                    for cc in range(NCC):
                        p.op("vector", lambda e, c=cc:
                             e.tensor_mul(f_sslab[:, c * 512:(c + 1) * 512],
                                          f_sslab[:, c * 512:(c + 1) * 512],
                                          f_abslab[:, 0:512]), inc=s_x2ap)
                        p.wait("vector", s_x2ap, s_x2ap.v)
                        p.op("vector", lambda e, c=cc:
                             e.tensor_add(f_sslab[:, c * 512:(c + 1) * 512],
                                          f_sslab[:, c * 512:(c + 1) * 512],
                                          f_abslab[:, 512:1024]), inc=s_x2ap)
                    p.wait("sync", s_x2ap, s_x2ap.v)
                    p.op("sync", lambda e, t=tc:
                         e.dma_start(
                             out=y[:, :, t * 512:(t + 1) * 512],
                             in_=f_sslab[:].rearrange("p (cc t) -> p cc t",
                                                    cc=NCC)),
                         inc=s_yst, n=16)
            p.wait("sync", s_yst, s_yst.v)

        with nc.Block() as block:
            p.flush(block)

    if verbose:
        print("instruction queue sizes:", p.counts())
    return nc



def _u8_of_i32(a):
    return a.view(np.uint8).reshape(*a.shape, 4)[..., 0]


def stage_inputs(inputs):
    x = np.asarray(inputs["L0_attn_x"])
    wqh = np.ascontiguousarray(np.asarray(inputs["L0_attn_wq"]).T)
    wkh = np.ascontiguousarray(np.asarray(inputs["L0_attn_wk"]).T)
    wvh = np.ascontiguousarray(np.asarray(inputs["L0_attn_wv"]).T)
    woh = np.ascontiguousarray(np.asarray(inputs["L0_attn_wo"]).T)
    wgT = np.ascontiguousarray(_u8_of_i32(np.asarray(inputs["L0_ffn_gate_w"])).T)
    wuT = np.ascontiguousarray(_u8_of_i32(np.asarray(inputs["L0_ffn_up_w"])).T)
    wdT = np.ascontiguousarray(_u8_of_i32(np.asarray(inputs["L0_ffn_down_w"])).T)
    gsc = np.ascontiguousarray(
        np.asarray(inputs["L0_ffn_gate_scale"]).reshape(NGI, 128).T)
    usc = np.ascontiguousarray(
        np.asarray(inputs["L0_ffn_up_scale"]).reshape(NGI, 128).T)
    dsc = np.ascontiguousarray(
        np.asarray(inputs["L0_ffn_down_scale"]).reshape(NCC, 128).T)

    j = np.arange(64, dtype=np.float64)
    inv = 10000.0 ** (-2.0 * j / 128.0)
    tk = np.arange(T, dtype=np.float64)
    ang = inv[:, None] * tk[None, :]
    ropek_np = np.empty((128, T), dtype=np.float32)
    ropek_np[0:64] = np.cos(ang)
    sn = np.sin(ang)
    sn[:32] *= -1.0
    ropek_np[64:128] = sn

    con = np.zeros((128, 4), dtype=np.float32)
    con[:, 0] = EXP_BIAS
    con[:, 1] = -128.0
    con[:, 2] = EPS
    onec_np = np.ones((128, 1), dtype=ml_dtypes.bfloat16)
    oner_np = np.ones((1, 128), dtype=np.float32)

    xts = [np.ascontiguousarray(x[b].T) for b in range(B)]
    in_maps = []
    for c in range(8):
        b, qb = c // 4, c % 4
        q0 = qb * TLOC
        in_maps.append({
            "xt": xts[b],
            "xq": np.ascontiguousarray(xts[b][:, q0:q0 + TLOC]),
            "wq": wqh, "wk": wkh, "wv": wvh, "wo": woh,
            "wg": wgT, "wu": wuT, "wd": wdT,
            "gs": gsc, "us": usc, "ds": dsc,
            "ropek": ropek_np,
            "ropeq": np.ascontiguousarray(ropek_np[:, q0:q0 + TLOC]),
            "consts": con, "onec": onec_np, "oner": oner_np,
        })
    return in_maps


def _tune_compiler_flags():
    try:
        import libneuronxla.libncc as ncc
        ncc.NEURON_CC_FLAGS = [
            f
            for f in ncc.NEURON_CC_FLAGS
        ]
    except Exception:
        pass


def kernel(**inputs):
    _tune_compiler_flags()
    in_maps = stage_inputs(inputs)
    nc = build_program()
    res = run_bass_kernel_spmd(nc, in_maps, core_ids=list(range(8)))
    out = np.empty((B, T, D), dtype=np.float32)
    for c in range(8):
        b, qb = c // 4, c % 4
        yv = np.asarray(res.results[c]["y"])
        blk = yv.transpose(1, 0, 2).reshape(D, TLOC)
        out[b, qb * TLOC:(qb + 1) * TLOC, :] = blk.T
    return out


if __name__ == "__main__":
    nc = build_program(verbose=True)
    n = sum(len(bb.instructions) for f in nc.m.functions for bb in f.blocks)
    print("total instructions:", n)
